# version 23
# baseline (speedup 1.0000x reference)
import numpy as np
import ml_dtypes
from contextlib import ExitStack

import concourse.bass as bass
import concourse.mybir as mybir
from concourse.bass_utils import run_bass_kernel_spmd

F32 = mybir.dt.float32
BF16 = mybir.dt.bfloat16
AF = mybir.ActivationFunctionType
ALU = mybir.AluOpType

B, S, D, HN, DH, FF, RK = 4, 4096, 2048, 16, 128, 8192, 32
CI = D // 128
TT = 512
TOWN = 2048
SCALE = float(1.0 / np.sqrt(128.0))
INV_D = float(1.0 / D)
EPS = 1e-5
NEG = -1.0e30

V_SB, V_L1S, V_L1B, V_L2S, V_L2B = 0, 1, 2, 3, 4

ENGS = ["sync", "gpsimd", "tensor", "vector", "scalar"]


class Sem:
    def __init__(self, nc, es, name):
        self.h = es.enter_context(nc.semaphore(name))
        self.name = name
        self.n = 0


class Plan:

    def __init__(self):
        self.q = {e: [] for e in ENGS}
        self.memo = {}

    def op(self, ek, build, waits=(), inc=None, inck=1):
        fw = []
        for s, t in waits:
            if not t or t <= 0:
                continue
            key = (ek, s.name)
            if self.memo.get(key, 0) >= t:
                continue
            self.memo[key] = t
            fw.append((s.h, t))
        tag = None
        if inc is not None:
            inc.n += inck
            tag = inc.n

        def fn(eng, fw=fw, build=build, inc=inc, inck=inck):
            for h, t in fw:
                eng.wait_ge(h, t)
            ins = build(eng)
            if inc is not None:
                ins.then_inc(inc.h, inck)

        self.q[ek].append(fn)
        return tag

    def run(self, nc):
        with nc.Block() as blk:
            for e in ENGS:
                fns = self.q[e]
                if fns:
                    getattr(blk, e)(lambda eng, fns=fns: [f(eng) for f in fns])


def drive(pl, gens, evt):
    norm = [(iter(g[0]), g[1]) if isinstance(g, tuple) else (iter(g), 1)
            for g in gens]
    active = [list(x) for x in norm]
    blocked = {}
    progress = True
    while active:
        if not progress:
            raise RuntimeError(
                f"drive deadlock: blocked on {set(blocked.values())}")
        progress = False
        for ent in list(active):
            g, w = ent
            if id(g) in blocked and blocked[id(g)] not in evt:
                continue
            blocked.pop(id(g), None)
            for _ in range(w):
                try:
                    r = next(g)
                    progress = True
                    if isinstance(r, tuple) and r and r[0] == 'need':
                        if r[1] not in evt:
                            blocked[id(g)] = r[1]
                            break
                except StopIteration:
                    active.remove(ent)
                    progress = True
                    break


def build_program(taps=()):
    nc = bass.Bass("TRN2", target_bir_lowering=False)
    es = ExitStack()
    T = {}
    uidc = [0]

    def uid(n):
        uidc[0] += 1
        return f"{n}_{uidc[0]}"

    def dram(name, shape, dtype, kind="Internal"):
        kd = "ExternalOutput" if name in taps else kind
        T[name] = nc.dram_tensor(name, shape, dtype, kind=kd).ap()
        return T[name]

    dram("xt", [D, S], F32, "ExternalInput")
    dram("xto", [D, TOWN], F32, "ExternalInput")
    dram("vecs", [128, 80], F32, "ExternalInput")
    dram("wrt", [128, 48], F32, "ExternalInput")
    dram("wdt", [128, 512], F32, "ExternalInput")
    dram("wupt", [RK, D], F32, "ExternalInput")
    dram("cmask", [128, 8 * 512], BF16, "ExternalInput")
    for nm in ["wq1", "wk1", "wv1", "wo1", "wq2", "wk2", "wv2", "wo2"]:
        dram(nm, [4 * 128, CI * 512], F32, "ExternalInput")
    dram("wf1", [16 * 128, CI * 512], F32, "ExternalInput")
    dram("wf2", [16 * 128, 64 * 128], F32, "ExternalInput")

    dram("xn16all", [D, S], BF16)
    dram("y16all", [D, S], BF16)
    dram("xn32own", [D, TOWN], F32)
    dram("xn16own", [D, TOWN], BF16)
    dram("y16own", [D, TOWN], BF16)
    dram("kt1", [D, S], BF16)
    dram("v1", [S, D], BF16)
    dram("qt1", [D, TOWN], BF16)
    dram("kt2", [D, S], BF16)
    dram("v2", [S, D], BF16)
    dram("qt2", [D, TOWN], BF16)
    dram("s32", [D, TOWN], F32)
    dram("u32", [D, TOWN], F32)
    dram("z16", [D, TOWN], BF16)
    dram("h16", [FF, TOWN], BF16)
    dram("wk1b", [4 * 128, CI * 512], BF16)
    dram("wv1b", [4 * 128, CI * 512], BF16)
    dram("wk2b", [4 * 128, CI * 512], BF16)
    dram("wv2b", [4 * 128, CI * 512], BF16)
    dram("wo1b", [4 * 128, CI * 512], BF16)
    dram("wo2b", [4 * 128, CI * 512], BF16)
    dram("wf1b", [16 * 128, CI * 512], BF16)
    dram("wf2b", [16 * 128, 64 * 128], BF16)
    dram("mrows", [1, 2 * TOWN], F32)
    dram("outT", [D, TOWN], F32, "ExternalOutput")

    semnames = (["dx", "dw", "ds", "mm", "ex", "ve", "ini", "d2d", "a1",
                 "a1b", "row", "st16", "st32", "sty", "fl"] +
                [f"cv{i}" for i in range(5)] +
                [f"xb{i}" for i in range(2)] + [f"ws{i}" for i in range(4)] +
                [f"ev{i}" for i in range(4)] + [f"ld{i}" for i in range(2)] +
                [f"xr{i}" for i in range(4)])
    SM = {s: Sem(nc, es, s) for s in semnames}
    dx, dw, ds, mm, ex, ve = (SM[s] for s in ["dx", "dw", "ds", "mm", "ex", "ve"])
    (S_INI, S_D2D, S_A1, S_A1B, S_ROW, S_ST16, S_ST32, S_STY,
     S_FL) = (SM[s] for s in ["ini", "d2d", "a1", "a1b", "row", "st16",
                              "st32", "sty", "fl"])
    S_CV = [SM[f"cv{i}"] for i in range(5)]
    S_XB = [SM[f"xb{i}"] for i in range(2)]
    S_WS = [SM[f"ws{i}"] for i in range(4)]
    S_EV = [SM[f"ev{i}"] for i in range(4)]
    S_LD = [SM[f"ld{i}"] for i in range(2)]
    S_XR = [SM[f"xr{i}"] for i in range(4)]
    ALLDMA = ([S_INI, S_D2D, S_A1, S_A1B, S_ROW, S_ST16, S_ST32, S_STY,
               S_FL] +
              S_CV + S_XB + S_WS + S_EV + S_LD + S_XR)

    def drain_waits():
        return [(s, s.n) for s in ALLDMA if s.n > 0]


    sb = lambda name, shape, dt: es.enter_context(nc.sbuf_tensor(name, shape, dt))
    WS = [sb("ws0", [128, 8192], BF16), sb("ws1", [128, 8192], BF16)]
    ONESC = sb("onesc", [128, 1], BF16)
    ONESR = sb("onesr", [1, 128], F32)
    EPSC = sb("epsc", [128, 1], F32)
    VECS = sb("vecsb", [128, 80], F32)
    WRT = sb("wrtb", [128, 48], F32)
    WDT = sb("wdtb", [128, 512], BF16)
    WUP = sb("wupb", [RK, D], BF16)
    CMK = sb("cmkb", [128, 8 * 512], BF16)
    IDN = sb("idn", [128, 128], BF16)
    T1 = sb("t1res", [RK, TOWN], BF16)
    ROWS = sb("rows", [1, 3584], F32)
    L3 = sb("l3", [3, 512], F32)
    PSW = [es.enter_context(nc.psum_tensor(f"psw{i}", [128, 1024], F32))
           for i in range(4)]

    R_MEAN, R_T2, R_MSQ, R_VAR, R_MR, R_L1, R_L2 = (
        0, 512, 1024, 1536, 2048, 2560, 3072)
    R_TMP = R_T2

    def rows(off, n=512):
        return ROWS[:, off:off + n]

    def bcast_ap(psw, half):
        return bass.AP(psw, 512 * half, [[1024, 128], [0, CI], [1, 512]])

    def wide_ap(psw):
        return bass.AP(psw, 0, [[1024, 128], [1, 1024]])

    pl = Plan()
    pl.op("gpsimd", lambda g: g.memset(ONESC[:, :], 1.0))
    pl.op("gpsimd", lambda g: g.memset(ONESR[:, :], 1.0))
    pl.op("gpsimd", lambda g: g.memset(EPSC[:, :], EPS))
    pl.op("gpsimd", lambda g: g.memset(IDN[:, :], 1.0))
    pl.op("gpsimd", lambda g: g.affine_select(IDN[:, :], IDN[:, :],
                                              [[1, 128]], ALU.is_equal, 0.0,
                                              base=0, channel_multiplier=-1))
    pl.op("gpsimd", lambda g: g.dma_start(WDT[:, :], T["wdt"]),
          inc=S_INI, inck=16)
    pl.op("gpsimd", lambda g: g.dma_start(WUP[:, :], T["wupt"]),
          inc=S_INI, inck=16)
    pl.op("sync", lambda sy: sy.dma_start(VECS[:, :], T["vecs"]),
          inc=S_INI, inck=16)
    pl.op("sync", lambda sy: sy.dma_start(WRT[:, :], T["wrt"]),
          inc=S_INI, inck=16)
    pl.op("sync", lambda sy: sy.dma_start(CMK[:, :], T["cmask"]),
          inc=S_INI, inck=16)
    pl.op("sync", lambda sy: sy.nop(), waits=[(S_INI, S_INI.n)])
    pl.run(nc)

    def gen_ln(pl, evt, tiles):
        esl = ExitStack()
        XB = esl.enter_context(nc.sbuf_tensor(uid("xb"), [128, 8192], F32))
        XN = esl.enter_context(nc.sbuf_tensor(uid("xnb"), [128, 8192], F32))
        B16 = esl.enter_context(nc.sbuf_tensor(uid("b16"), [128, 16384], BF16))
        evt.setdefault("_esl", []).append(esl)
        x16 = B16[:, 0:8192]
        sq16 = B16[:, 8192:16384]
        x16v = x16.rearrange("p (c t) -> p c t", c=CI)
        sq16v = sq16.rearrange("p (c t) -> p c t", c=CI)
        xnv = XN[:, :].rearrange("p (c t) -> p c t", c=CI)

        xb_gate = []
        xn_gate = []
        x16_gate = []
        sq_gate = []
        rowb_gate = []
        bc_gate = []
        msk_prev = 0

        for job in tiles:
            src, i = job["src"], job["i"]
            out16, out32 = job["out16"], job.get("out32")
            sb_cols = job.get("sb_cols")
            second = job.get("second")
            router = job.get("router", False)
            name = job["name"]
            csl = slice(TT * i, TT * (i + 1))
            t_ld = pl.op("sync", lambda sy, src=src, csl=csl: sy.dma_start(
                XB[:, :].rearrange("p (c t) -> p c t", c=CI),
                src[:, csl].rearrange("(c p) t -> p c t", p=128)),
                waits=xb_gate, inc=S_XB[0], inck=16)
            t_x16 = pl.op("scalar", lambda sc: sc.activation(
                x16, XB[:, :], AF.Copy),
                waits=[(S_XB[0], t_ld)] + x16_gate, inc=ex)
            t_sq = pl.op("scalar", lambda sc: sc.activation(
                sq16, XB[:, :], AF.Square), waits=sq_gate, inc=ex)
            yield

            def fsum(t):
                for c in range(CI):
                    t.matmul(PSW[2][0:1, 0:512], ONESC[:, :], x16v[:, c, :],
                             start=(c == 0), stop=(c == CI - 1))
                for c in range(CI):
                    ins = t.matmul(PSW[2][0:1, 512:1024], ONESC[:, :],
                                   sq16v[:, c, :], start=(c == 0),
                                   stop=(c == CI - 1))
                return ins
            t_sum = pl.op("tensor", fsum,
                          waits=[(ex, t_x16), (ex, t_sq)] + rowb_gate, inc=mm)

            def frows(v):
                v.tensor_scalar(rows(R_MEAN), PSW[2][0:1, 0:512], INV_D, None,
                                ALU.mult)
                v.tensor_scalar(rows(R_T2), PSW[2][0:1, 512:1024], INV_D, None,
                                ALU.mult)
                v.tensor_mul(rows(R_MSQ), rows(R_MEAN), rows(R_MEAN))
                return v.tensor_sub(rows(R_VAR), rows(R_T2), rows(R_MSQ))
            t_rows = pl.op("vector", frows, waits=[(mm, t_sum)], inc=ve)
            rowb_gate = [(ve, t_rows)]

            def frstd(sc):
                sc.activation(rows(R_T2), rows(R_VAR), AF.Ln,
                              bias=EPSC[0:1, 0:1])
                return sc.activation(rows(R_MSQ), rows(R_T2), AF.Exp,
                                     scale=-0.5)
            t_rstd = pl.op("scalar", frstd, waits=[(ve, t_rows)], inc=ex)

            def fmr(v, second=second):
                ins = v.tensor_mul(rows(R_MR), rows(R_MEAN), rows(R_MSQ))
                if second is not None:
                    v.tensor_mul(rows(R_T2), rows(R_MSQ), rows(R_MSQ))
                    ins = v.tensor_mul(rows(R_T2), rows(R_VAR), rows(R_T2))
                return ins
            t_mr = pl.op("vector", fmr, waits=[(ex, t_rstd)], inc=ve)

            def fbc(t):
                t.matmul(PSW[3][:, 0:512], ONESR[:, :], rows(R_MSQ),
                         start=True, stop=True)
                return t.matmul(PSW[3][:, 512:1024], ONESR[:, :], rows(R_MR),
                                start=True, stop=True)
            t_bc = pl.op("tensor", fbc, waits=[(ve, t_mr)] + bc_gate, inc=mm)

            def fxn(v):
                xbv = XB[:, :].rearrange("p (c t) -> p c t", c=CI)
                v.tensor_mul(xnv, xbv, bcast_ap(PSW[3], 0))
                return v.tensor_sub(xnv, xnv, bcast_ap(PSW[3], 1))
            t_xn = pl.op("vector", fxn, waits=[(mm, t_bc)] + xn_gate, inc=ve)
            bc_gate = [(ve, t_xn)]
            yield

            if sb_cols is None:
                t_o16 = pl.op("scalar", lambda sc: sc.activation(
                    x16, XN[:, :], AF.Copy),
                    waits=[(ve, t_xn), (mm, t_sum)], inc=ex)
                o16_sem = ex
            else:
                scol, bcol = sb_cols
                def fo16(v, scol=scol, bcol=bcol):
                    for c in range(CI):
                        ins = v.tensor_scalar(
                            x16[:, TT * c:TT * (c + 1)],
                            XN[:, TT * c:TT * (c + 1)],
                            VECS[:, scol * 16 + c:scol * 16 + c + 1],
                            VECS[:, bcol * 16 + c:bcol * 16 + c + 1],
                            ALU.mult, ALU.add)
                    return ins
                t_o16 = pl.op("vector", fo16,
                              waits=[(ve, t_xn), (mm, t_sum)], inc=ve)
                o16_sem = ve
            t_st16 = pl.op("scalar", lambda sc, out16=out16, csl=csl:
                           sc.dma_start(
                out16[:, csl].rearrange("(c p) t -> p c t", p=128), x16v),
                waits=[(o16_sem, t_o16)], inc=S_ST16, inck=16)
            x16_gate = [(S_ST16, t_st16)]
            xn_gate = [(o16_sem, t_o16)]
            evt[("st16", name, i)] = (S_ST16, t_st16)
            if out32 is not None:
                t_st32 = pl.op("scalar", lambda sc, out32=out32, csl=csl:
                               sc.dma_start(
                    out32[:, csl].rearrange("(c p) t -> p c t", p=128), xnv),
                    waits=[(ve, t_xn)], inc=S_ST32, inck=16)
                xn_gate.append((S_ST32, t_st32))
                evt[("st32", name, i)] = (S_ST32, t_st32)
            yield

            if router:
                def frt(t):
                    for c in range(CI):
                        ins = t.matmul(PSW[2][0:3, 0:512],
                                       WRT[:, 3 * c:3 * (c + 1)],
                                       XN[:, TT * c:TT * (c + 1)],
                                       start=(c == 0), stop=(c == CI - 1))
                    return ins
                t_rt = pl.op("tensor", frt,
                             waits=[(ve, t_xn), (ve, t_rows)], inc=mm)
                xn_gate.append((mm, t_rt))
                t_rl = pl.op("scalar", lambda sc: sc.copy(
                    L3[:, :], PSW[2][0:3, 0:512]), waits=[(mm, t_rt)], inc=ex)
                pl.op("sync", lambda sy: sy.dma_start(rows(R_L1), L3[1:2, :]),
                      waits=[(ex, t_rl), (ve, msk_prev)], inc=S_ROW, inck=16)
                t_rmv = pl.op("sync", lambda sy: sy.dma_start(
                    rows(R_L2), L3[2:3, :]), inc=S_ROW, inck=16)

                def fmask(v):
                    v.tensor_tensor(rows(R_MEAN), L3[0:1, :], rows(R_L1),
                                    ALU.is_ge)
                    v.tensor_tensor(rows(R_VAR), L3[0:1, :], rows(R_L2),
                                    ALU.is_ge)
                    v.tensor_mul(rows(R_MEAN), rows(R_MEAN), rows(R_VAR))
                    v.tensor_tensor(rows(R_VAR), rows(R_L1), rows(R_L2),
                                    ALU.is_ge)
                    v.tensor_scalar(rows(R_MSQ), rows(R_MEAN), -1.0, 1.0,
                                    ALU.mult, ALU.add)
                    return v.tensor_mul(rows(R_VAR), rows(R_MSQ), rows(R_VAR))
                t_msk = pl.op("vector", fmask,
                              waits=[(S_ROW, t_rmv), (ex, t_rl)], inc=ve)
                t_ms = pl.op("scalar", lambda sc, i=i: sc.dma_start(
                    T["mrows"][0:1, TT * i:TT * (i + 1)], rows(R_MEAN)),
                    waits=[(ve, t_msk)], inc=S_ROW, inck=16)
                t_ms = pl.op("scalar", lambda sc, i=i: sc.dma_start(
                    T["mrows"][0:1, TOWN + TT * i:TOWN + TT * (i + 1)],
                    rows(R_VAR)), inc=S_ROW, inck=16)
                evt[("mrow", i)] = (S_ROW, t_ms)
                msk_prev = t_msk
                rowb_gate = [(ve, t_msk), (ex, t_rl)]
                yield

            if second is not None:
                def frstd2(sc):
                    sc.activation(rows(R_T2), rows(R_T2), AF.Ln,
                                  bias=EPSC[0:1, 0:1])
                    return sc.activation(rows(R_T2), rows(R_T2), AF.Exp,
                                         scale=-0.5)
                t_rstd2 = pl.op("scalar", frstd2, waits=[(ve, t_mr)], inc=ex)
                t_bc2 = pl.op("tensor", lambda t: t.matmul(
                    PSW[3][:, 0:512], ONESR[:, :], rows(R_T2),
                    start=True, stop=True),
                    waits=[(ex, t_rstd2), (ve, t_xn)], inc=mm)
                def fy(v):
                    return v.tensor_mul(
                        XB[:, :].rearrange("p (c t) -> p c t", c=CI),
                        xnv, bcast_ap(PSW[3], 0))
                t_y = pl.op("vector", fy, waits=[(mm, t_bc2)], inc=ve)
                bc_gate = [(ve, t_y)]
                scol, bcol = second["scale_col"], second["bias_col"]

                def fy16(v, scol=scol, bcol=bcol):
                    for c in range(CI):
                        ins = v.tensor_scalar(
                            sq16[:, TT * c:TT * (c + 1)],
                            XB[:, TT * c:TT * (c + 1)],
                            VECS[:, scol * 16 + c:scol * 16 + c + 1],
                            VECS[:, bcol * 16 + c:bcol * 16 + c + 1],
                            ALU.mult, ALU.add)
                    return ins
                t_y16 = pl.op("vector", fy16,
                              waits=[(ve, t_y), (mm, t_sum)], inc=ve)
                t_sty = pl.op("scalar", lambda sc, second=second, csl=csl:
                              sc.dma_start(
                    second["out"][:, csl].rearrange("(c p) t -> p c t", p=128),
                    sq16v), waits=[(ve, t_y16)], inc=S_STY, inck=16)
                evt[("sty", second["name"], i)] = (S_STY, t_sty)
                xb_gate = [(ve, t_y16)]
                sq_gate = [(S_STY, t_sty)]
            else:
                xb_gate = [(ve, t_xn), (ex, t_sq)]
                sq_gate = [(mm, t_sum)]
            yield
        evt[("ln_done", id(tiles))] = True

    def gen_proj(pl, evt, groups, a1_elems=32768, slots=(0, 1, 2, 3),
                 ev_sems=None, a1_sem=None, sh_ws=None, ws_bufs=None,
                 coarse=False, a1_n=2):
        if ev_sems is None:
            ev_sems = S_EV
        if a1_sem is None:
            a1_sem = S_A1
        if ws_bufs is None:
            ws_bufs = [(WS[0], S_WS[0]), (WS[1], S_WS[1])]
        sh_ws = dict(ws_gate=[[] for _ in ws_bufs], wseq=[0])
        nslot = len(slots)
        nev = len(ev_sems)
        esl = ExitStack()
        A1B_ = [esl.enter_context(nc.sbuf_tensor(uid("a1"), [128, a1_elems],
                                                 BF16)) for _ in range(a1_n)]
        A1S_ = ([a1_sem] if a1_n == 1 else [S_A1, S_A1B])[:a1_n]
        EV = esl.enter_context(nc.sbuf_tensor(uid("ev"), [128, nev * 512],
                                              BF16))
        evt.setdefault("_esl", []).append(esl)
        seg_tok = a1_elems // CI
        ws_gate = sh_ws["ws_gate"]
        ev_gate = [[] for _ in range(nev)]
        slot_gate = [[] for _ in range(nslot)]
        a1_gate = [[] for _ in range(a1_n)]
        a1_last = [0] * a1_n
        aseq = [0]
        sseq = [0]
        eseq = [0]
        wseq = sh_ws["wseq"]

        def psum_slot(k):
            i = slots[k]
            return PSW[i // 2][:, 512 * (i % 2):512 * (i % 2 + 1)]

        def evac_store(t_mm, make_store, gelu=False):
            ei = eseq[0] % nev
            eseq[0] += 1
            sl_i = (sseq[0] - 1) % nslot
            sl = psum_slot(sl_i)
            if gelu or ei % 2:
                t_ev = pl.op("scalar", lambda sc, sl=sl, ei=ei: sc.activation(
                    EV[:, 512 * ei:512 * (ei + 1)], sl,
                    AF.Gelu_apprx_tanh if gelu else AF.Copy),
                    waits=[(mm, t_mm)] + ev_gate[ei], inc=ex)
                esem = ex
            else:
                t_ev = pl.op("vector", lambda v, sl=sl, ei=ei: v.tensor_copy(
                    EV[:, 512 * ei:512 * (ei + 1)], sl),
                    waits=[(mm, t_mm)] + ev_gate[ei], inc=ve)
                esem = ve
            slot_gate[sl_i] = [(esem, t_ev)]
            t_st = pl.op("scalar", lambda sc, ei=ei: make_store(
                sc, EV[:, 512 * ei:512 * (ei + 1)]),
                waits=[(esem, t_ev)], inc=ev_sems[ei], inck=16)
            ev_gate[ei] = [(ev_sems[ei], t_st)]
            return t_st

        for grp in groups:
            act, a0, an = grp["act"], grp["a0"], grp["an"]
            for key in grp.get("gates", ()):
                while key not in evt:
                    yield ('need', key)
            ggates = [evt[k] for k in grp.get("gates", ())]
            for seg in range((an + seg_tok - 1) // seg_tok):
                ai = aseq[0] % a1_n
                aseq[0] += 1
                A1 = A1B_[ai]
                asem = A1S_[ai]
                sa0 = a0 + seg_tok * seg
                sn = min(seg_tok, a0 + an - sa0)
                nct = sn // TT
                t_a1 = pl.op("sync", lambda sy, act=act, sa0=sa0, sn=sn,
                             A1=A1: sy.dma_start(
                    A1[:, 0:CI * sn].rearrange("p (c t) -> p c t", c=CI),
                    act[:, sa0:sa0 + sn].rearrange("(c p) t -> p c t", p=128)),
                    waits=a1_gate[ai] + ggates, inc=asem, inck=16)
                for job in grp["jobs"]:
                    if job.get("wkey"):
                        while job["wkey"] not in CVT:
                            yield ('need', ("cvt", job["wkey"]))
                        job["wgate"] = [CVT[job["wkey"]]]
                    if job["kind"] == "rank":
                        for tt_i in range(nct):
                            sl_i = sseq[0] % nslot
                            sl = psum_slot(sl_i)
                            sseq[0] += 1

                            def fr(t, tt_i=tt_i, sl=sl, sn=sn, A1=A1):
                                for c in range(CI):
                                    ins = t.matmul(
                                        sl[0:RK, :],
                                        WDT[:, RK * c:RK * (c + 1)],
                                        A1[:, c * sn + TT * tt_i:
                                           c * sn + TT * (tt_i + 1)],
                                        start=(c == 0), stop=(c == CI - 1))
                                return ins
                            t_mm = pl.op("tensor", fr,
                                         waits=[(asem, t_a1)] +
                                         slot_gate[sl_i], inc=mm)
                            a1_last[ai] = t_mm
                            t_ev = pl.op("vector",
                                         lambda v, sl=sl, tt_i=tt_i, sa0=sa0,
                                         a0=a0: v.tensor_copy(
                                T1[:, sa0 - a0 + TT * tt_i:
                                   sa0 - a0 + TT * (tt_i + 1)],
                                sl[0:RK, :]), waits=[(mm, t_mm)], inc=ve)
                            slot_gate[sl_i] = [(ve, t_ev)]
                            yield
                        continue
                    for st in range(job["nstrip"]):
                        wp = wseq[0] % len(ws_bufs)
                        wseq[0] += 1
                        wbuf, wsem = ws_bufs[wp]
                        wdram = job["w"]
                        if job.get("wbf16"):
                            t_w = pl.op("sync",
                                        lambda sy, wbuf=wbuf, st=st,
                                        wdram=wdram: sy.dma_start(
                                wbuf[:, :],
                                wdram[128 * st:128 * (st + 1), :]),
                                waits=ws_gate[wp] + job.get("wgate", []),
                                inc=wsem, inck=16)
                        else:
                            t_w = pl.op("gpsimd",
                                        lambda g, wbuf=wbuf, st=st,
                                        wdram=wdram: g.dma_start(
                                wbuf[:, :],
                                wdram[128 * st:128 * (st + 1), :]),
                                waits=ws_gate[wp], inc=wsem, inck=16)
                        last_mm = 0
                        if job["kind"] == "feat":
                            for tt_i in range(nct):
                                for oc in range(4):
                                    sl_i = sseq[0] % nslot
                                    sl = psum_slot(sl_i)
                                    sseq[0] += 1

                                    def fm(t, wbuf=wbuf, tt_i=tt_i, oc=oc,
                                           sl=sl, sn=sn, A1=A1):
                                        for c in range(CI):
                                            ins = t.matmul(
                                                sl,
                                                wbuf[:, 512 * c + 128 * oc:
                                                     512 * c +
                                                     128 * (oc + 1)],
                                                A1[:, c * sn + TT * tt_i:
                                                   c * sn + TT * (tt_i + 1)],
                                                start=(c == 0),
                                                stop=(c == CI - 1))
                                        return ins
                                    t_mm = pl.op("tensor", fm,
                                                 waits=[(wsem, t_w),
                                                        (asem, t_a1)] +
                                                 slot_gate[sl_i], inc=mm)
                                    last_mm = t_mm
                                    a1_last[ai] = t_mm
                                    ws_gate[wp] = [(mm, t_mm)]
                                    orow = 128 * (4 * st + oc)
                                    ocol = sa0 - a0 + TT * tt_i + \
                                        job.get("ocol0", a0)
                                    out = job["out"]
                                    evac_store(
                                        t_mm,
                                        lambda sc, ev, orow=orow, ocol=ocol,
                                        out=out: sc.dma_start(
                                            out[orow:orow + 128,
                                                ocol:ocol + TT], ev),
                                        gelu=job.get("gelu", False))
                                    if not coarse:
                                        yield
                        else:
                            for tt8 in range(sn // 128):
                                sl_i = sseq[0] % nslot
                                sl = psum_slot(sl_i)
                                sseq[0] += 1

                                def fm(t, wbuf=wbuf, tt8=tt8, sl=sl, sn=sn,
                                       A1=A1):
                                    for c in range(CI):
                                        ins = t.matmul(
                                            sl,
                                            A1[:, c * sn + 128 * tt8:
                                               c * sn + 128 * (tt8 + 1)],
                                            wbuf[:, 512 * c:512 * (c + 1)],
                                            start=(c == 0),
                                            stop=(c == CI - 1))
                                    return ins
                                t_mm = pl.op("tensor", fm,
                                             waits=[(wsem, t_w),
                                                    (asem, t_a1)] +
                                             slot_gate[sl_i], inc=mm)
                                last_mm = t_mm
                                a1_last[ai] = t_mm
                                ws_gate[wp] = [(mm, t_mm)]
                                trow = sa0 + 128 * tt8
                                out = job["out"]
                                evac_store(
                                    t_mm,
                                    lambda sc, ev, trow=trow, st=st, out=out:
                                    sc.dma_start(
                                        out[trow:trow + 128,
                                            512 * st:512 * (st + 1)], ev))
                                if not coarse:
                                    yield
                        ws_gate[wp] = [(mm, last_mm)]
                        if coarse:
                            yield
                a1_gate[ai] = [(mm, a1_last[ai])]

    def run_merged(gens_f):
        pl = Plan()
        evt = {}
        EVT_G[0] = evt
        drive(pl, gens_f(pl, evt), evt)
        pl.op("sync", lambda sy: sy.nop(), waits=drain_waits())
        pl.run(nc)
        for esl in reversed(evt.get("_esl", [])):
            esl.close()

    CVT = {}
    EVT_G = [{}]

    def cv_mk(pl, name, dst, srcn, sem, pieces=2):
        rows_ = {"wk1b": 512, "wv1b": 512, "wk2b": 512, "wv2b": 512,
                 "wo1b": 512, "wo2b": 512, "wf1b": 2048, "wf2b": 2048}[dst]
        step = rows_ // pieces
        t = None
        for i in range(pieces):
            t = pl.op("gpsimd", lambda g, i=i, dst=dst, srcn=srcn,
                      step=step: g.dma_start(
                T[dst][step * i:step * (i + 1), :],
                T[srcn][step * i:step * (i + 1), :]), inc=sem, inck=16)
            yield
        CVT[name] = (sem, t)
        EVT_G[0][("cvt", name)] = True

    def gen_d2d(pl, evt):
        def cv(name, dst, srcn, sem, pieces=2):
            n = T[dst].shape[0] if False else None
            rows_ = {"wk1b": 512, "wv1b": 512, "wk2b": 512, "wv2b": 512,
                     "wo1b": 512, "wo2b": 512, "wf1b": 2048, "wf2b": 2048}[dst]
            step = rows_ // pieces
            t = None
            for i in range(pieces):
                t = pl.op("gpsimd", lambda g, i=i, dst=dst, srcn=srcn,
                          step=step: g.dma_start(
                    T[dst][step * i:step * (i + 1), :],
                    T[srcn][step * i:step * (i + 1), :]), inc=sem, inck=16)
                yield
            CVT[name] = (sem, t)
            EVT_G[0][("cvt", name)] = True
        yield from cv("wk1", "wk1b", "wk1", S_CV[0])
        yield from cv("wv1", "wv1b", "wv1", S_CV[1])

    def gen_d2d_b(pl, evt):
        yield from cv_mk(pl, "wk2", "wk2b", "wk2", S_CV[2])
        yield from cv_mk(pl, "wv2", "wv2b", "wv2", S_CV[3])
        yield from cv_mk(pl, "wo1", "wo1b", "wo1", S_D2D)

    def gen_d2d_b4(pl, evt):
        yield from cv_mk(pl, "wo2", "wo2b", "wo2", S_D2D)
        yield from cv_mk(pl, "wf1", "wf1b", "wf1", S_CV[4], pieces=8)

    def gen_d2d_c(pl, evt):
        yield from cv_mk(pl, "wf2", "wf2b", "wf2", S_D2D, pieces=8)

    def phase_a(pl, evt):
        sec_a = dict(out=T["y16all"], scale_col=V_L1S, bias_col=V_L1B,
                     name="ya")
        sec_o = dict(out=T["y16own"], scale_col=V_L1S, bias_col=V_L1B,
                     name="yo")
        tiles = ([dict(src=T["xt"], i=i, out16=T["xn16all"], name="xa",
                       second=sec_a) for i in range(8)] +
                 [dict(src=T["xto"], i=i, out16=T["xn16own"], name="xo",
                       out32=T["xn32own"], router=True, second=sec_o)
                  for i in range(4)])
        groups = []
        for s in range(4):
            groups.append(dict(
                act=T["xn16all"], a0=1024 * s, an=1024,
                gates=[("st16", "xa", 2 * s), ("st16", "xa", 2 * s + 1)],
                jobs=[dict(kind="feat", w=T["wk1b"], nstrip=4, out=T["kt1"],
                           wbf16=True, wkey="wk1", ocol0=1024 * s),
                      dict(kind="tok", w=T["wv1b"], nstrip=4, out=T["v1"],
                           wbf16=True, wkey="wv1")]))
        groups.append(dict(
            act=T["xn16own"], a0=0, an=TOWN,
            gates=[("st16", "xo", i) for i in range(4)],
            jobs=[dict(kind="feat", w=T["wq1"], nstrip=4, out=T["qt1"],
                       ocol0=0),
                  dict(kind="rank")]))
        return [(gen_ln(pl, evt, tiles), 1),
                (gen_proj(pl, evt, groups, a1_elems=16384, a1_n=1), 6),
                (gen_d2d(pl, evt), 1)]

    run_merged(phase_a)
    D2D = [S_D2D.n]


    def gen_attn(pl, evt, sh, qt, kt, vt, wob, out_s32, wob_key=None,
                 ws_bufs=None, coarse=False):
        esl = ExitStack()
        KB = esl.enter_context(nc.sbuf_tensor(uid("kb"), [128, 8192], BF16))
        VB = esl.enter_context(nc.sbuf_tensor(uid("vb"), [128, 8192], BF16))
        QB = esl.enter_context(nc.sbuf_tensor(uid("qb"), [128, 1024], BF16))
        PT = [esl.enter_context(nc.sbuf_tensor(uid("pt"), [128, 1024], BF16))
              for i in range(2)]
        CTX = esl.enter_context(nc.sbuf_tensor(uid("ctx"), [128, 8192], BF16))
        BCS = esl.enter_context(nc.sbuf_tensor(uid("bcs"), [128, 1024], F32))
        EVF = esl.enter_context(nc.sbuf_tensor(uid("evf"), [128, 1024], F32))
        XR = (esl.enter_context(nc.sbuf_tensor(uid("xr"), [128, 1024], F32))
              if not out_s32 else None)
        evt.setdefault("_esl", []).append(esl)
        if ws_bufs is None:
            ws_bufs = [(WS[0], S_WS[0]), (WS[1], S_WS[1])]
        ws_gate = [[] for _ in ws_bufs]
        wseq = [0]

        bank = {i: [] for i in range(4)}
        kqv_gate = []
        pt_gate = [[], []]
        ctxb_gate = [[], []]
        den_gate = []
        evf_gate = [[], []]
        xr_gate = [[], []]
        bcs_gate = []
        ctx_sb_gate = []

        for tk_i in range(4):
            nk = 8 * (tk_i + 1)
            last_cm = 0
            for pr in range(8):
                h0, h1 = 2 * pr, 2 * pr + 1

                def fk(sy, h0=h0, h1=h1, nk=nk, tk_i=tk_i):
                    sy.dma_start(KB[:, 0:128 * nk],
                                 kt[128 * h0:128 * (h0 + 1), 0:128 * nk]
                                 ).then_inc(S_LD[0].h, 16)
                    sy.dma_start(KB[:, 4096:4096 + 128 * nk],
                                 kt[128 * h1:128 * (h1 + 1), 0:128 * nk]
                                 ).then_inc(S_LD[0].h, 16)
                    sy.dma_start(QB[:, 0:512],
                                 qt[128 * h0:128 * (h0 + 1),
                                    TT * tk_i:TT * (tk_i + 1)]
                                 ).then_inc(S_LD[0].h, 16)
                    return sy.dma_start(QB[:, 512:1024],
                                        qt[128 * h1:128 * (h1 + 1),
                                           TT * tk_i:TT * (tk_i + 1)])
                S_LD[0].n += 48
                t_k = pl.op("sync", fk, waits=kqv_gate, inc=S_LD[0], inck=16)
                t_v = pl.op("sync", lambda sy, pr=pr, nk=nk: sy.dma_start(
                    VB[:, 0:256 * nk].rearrange("p (j f) -> p j f", j=nk),
                    vt[0:128 * nk, 256 * pr:256 * (pr + 1)].rearrange(
                        "(j p) f -> p j f", p=128)), inc=S_LD[0], inck=16)

                t_exp = [0] * nk
                t_pv = [0] * nk
                t_s = [0] * nk
                for j in range(nk + 1):
                    if j < nk:
                        sp = j % 2
                        d = j - (nk - 8)

                        def fs(t, j=j, sp=sp, d=d):
                            if d >= 0:
                                t.matmul(PSW[sp][:, 0:512], IDN[:, :],
                                         CMK[:, 512 * d:512 * (d + 1)],
                                         start=True, stop=False)
                                t.matmul(PSW[sp][:, 512:1024], IDN[:, :],
                                         CMK[:, 512 * d:512 * (d + 1)],
                                         start=True, stop=False)
                            t.matmul(PSW[sp][:, 0:512],
                                     KB[:, 128 * j:128 * (j + 1)],
                                     QB[:, 0:512], start=(d < 0), stop=True)
                            return t.matmul(
                                PSW[sp][:, 512:1024],
                                KB[:, 4096 + 128 * j:4096 + 128 * (j + 1)],
                                QB[:, 512:1024], start=(d < 0), stop=True)
                        t_s[j] = pl.op("tensor", fs,
                                       waits=[(S_LD[0], t_k)] +
                                       bank[2 * sp] + bank[2 * sp + 1],
                                       inc=mm)
                        ewait = [(mm, t_s[j])]
                        jp = j % 2
                        t_exp[j] = pl.op("scalar", lambda sc, jp=jp, sp=sp:
                                         sc.activation(PT[jp][:, :],
                                                       PSW[sp][:, :], AF.Exp,
                                                       scale=SCALE),
                                         waits=ewait + pt_gate[jp], inc=ex)
                        bank[2 * sp] = bank[2 * sp + 1] = [(ex, t_exp[j])]
                    jj = j - 1
                    if jj < 0:
                        if not coarse:
                            yield
                        continue
                    jp = jj % 2

                    def fpv(t, jj=jj, jp=jp, nk=nk):
                        t.matmul(PSW[2][:, 0:512],
                                 VB[:, 256 * jj:256 * jj + 128],
                                 PT[jp][:, 0:512], start=(jj == 0),
                                 stop=(jj == nk - 1))
                        t.matmul(PSW[2][:, 512:1024],
                                 VB[:, 256 * jj + 128:256 * (jj + 1)],
                                 PT[jp][:, 512:1024], start=(jj == 0),
                                 stop=(jj == nk - 1))
                        t.matmul(PSW[3][0:1, 0:512], ONESC[:, :],
                                 PT[jp][:, 0:512], start=(jj == 0),
                                 stop=(jj == nk - 1), skip_group_check=True)
                        return t.matmul(PSW[3][32:33, 0:512],
                                        ONESC[:, :], PT[jp][:, 512:1024],
                                        start=(jj == 0), stop=(jj == nk - 1),
                                        tile_position=(0, 32),
                                        skip_group_check=True)
                    w0 = [(ex, t_exp[jj]), (S_LD[0], t_v)]
                    if jj == 0:
                        w0 += ctxb_gate[0] + ctxb_gate[1] + den_gate
                    t_pv[jj] = pl.op("tensor", fpv, waits=w0, inc=mm)
                    pt_gate[jp] = [(mm, t_pv[jj])]
                    if not coarse:
                        yield
                kqv_gate = [(mm, t_pv[nk - 1])]

                def frc(sc):
                    sc.activation(rows(R_L1), PSW[3][0:1, 0:512], AF.Ln)
                    sc.activation(rows(R_L2), PSW[3][32:33, 0:512], AF.Ln)
                    sc.activation(rows(R_L1), rows(R_L1), AF.Exp, scale=-1.0)
                    return sc.activation(rows(R_L2), rows(R_L2), AF.Exp,
                                         scale=-1.0)
                t_rc = pl.op("scalar", frc, waits=[(mm, t_pv[nk - 1])], inc=ex)
                den_gate = [(ex, t_rc)]

                def fbc2(t):
                    t.matmul(PSW[0][:, 0:512], ONESR[:, :], rows(R_L1),
                             start=True, stop=True)
                    return t.matmul(PSW[0][:, 512:1024], ONESR[:, :],
                                    rows(R_L2), start=True, stop=True)
                t_bc = pl.op("tensor", fbc2,
                             waits=[(ex, t_rc)] + bank[0] + bank[1], inc=mm)
                t_bcc = pl.op("scalar", lambda sc: sc.copy(
                    BCS[:, :], PSW[0][:, :]),
                    waits=[(mm, t_bc)] + bcs_gate, inc=ex)
                bank[0] = bank[1] = [(ex, t_bcc)]

                def fcm(v, h0=h0, h1=h1):
                    v.tensor_mul(CTX[:, 512 * h0:512 * (h0 + 1)],
                                 PSW[2][:, 0:512], BCS[:, 0:512])
                    return v.tensor_mul(CTX[:, 512 * h1:512 * (h1 + 1)],
                                        PSW[2][:, 512:1024],
                                        BCS[:, 512:1024])
                t_cm = pl.op("vector", fcm,
                             waits=[(ex, t_bcc), (mm, t_pv[nk - 1])] +
                             ctx_sb_gate, inc=ve)
                ctxb_gate[0] = ctxb_gate[1] = [(ve, t_cm)]
                bcs_gate = [(ve, t_cm)]
                last_cm = t_cm
                yield

            oproj_last = 0
            if wob_key is not None:
                while wob_key not in CVT:
                    yield ('need', ("cvt", wob_key))
                sh["d2d"] = CVT[wob_key][1]
            for st in range(4):
                wp = wseq[0] % len(ws_bufs)
                wseq[0] += 1
                wbuf, wsem = ws_bufs[wp]
                t_w = pl.op("sync", lambda sy, wbuf=wbuf, st=st: sy.dma_start(
                    wbuf[:, :], wob[128 * st:128 * (st + 1), :]),
                    waits=ws_gate[wp] + [(S_D2D, sh["d2d"])],
                    inc=wsem, inck=16)
                for oc in range(4):
                    si = (st * 4 + oc) % 4
                    sl = PSW[si // 2][:, 512 * (si % 2):512 * (si % 2 + 1)]

                    def fo(t, wbuf=wbuf, oc=oc, sl=sl):
                        for c in range(CI):
                            ins = t.matmul(
                                sl, wbuf[:, 512 * c + 128 * oc:
                                         512 * c + 128 * (oc + 1)],
                                CTX[:, 512 * c:512 * (c + 1)],
                                start=(c == 0), stop=(c == CI - 1))
                        return ins
                    t_mm = pl.op("tensor", fo,
                                 waits=[(wsem, t_w), (ve, last_cm)] +
                                 bank[si], inc=mm)
                    oproj_last = t_mm
                    ws_gate[wp] = [(mm, t_mm)]
                    ei = (st * 4 + oc) % 2
                    orow = 128 * (4 * st + oc)
                    ocol = TT * tk_i
                    if out_s32:
                        t_ev = pl.op("vector", lambda v, sl=sl, ei=ei:
                                     v.tensor_copy(
                            EVF[:, 512 * ei:512 * (ei + 1)], sl),
                            waits=[(mm, t_mm)] + evf_gate[ei], inc=ve)
                        dst = T["s32"]
                    else:
                        t_x = pl.op("sync", lambda sy, ei=ei, orow=orow,
                                    ocol=ocol: sy.dma_start(
                            XR[:, 512 * ei:512 * (ei + 1)],
                            T["xn32own"][orow:orow + 128, ocol:ocol + 512]),
                            waits=xr_gate[ei], inc=S_XR[ei], inck=16)
                        t_ev = pl.op("vector", lambda v, sl=sl, ei=ei:
                                     v.tensor_add(
                            EVF[:, 512 * ei:512 * (ei + 1)], sl,
                            XR[:, 512 * ei:512 * (ei + 1)]),
                            waits=[(mm, t_mm), (S_XR[ei], t_x)] +
                            evf_gate[ei], inc=ve)
                        xr_gate[ei] = [(ve, t_ev)]
                        dst = T["u32"]
                    bank[si] = [(ve, t_ev)]
                    t_st = pl.op("scalar", lambda sc, ei=ei, orow=orow,
                                 ocol=ocol, dst=dst: sc.dma_start(
                        dst[orow:orow + 128, ocol:ocol + 512],
                        EVF[:, 512 * ei:512 * (ei + 1)]),
                        waits=[(ve, t_ev)], inc=S_EV[ei], inck=16)
                    evf_gate[ei] = [(S_EV[ei], t_st)]
                    yield
            ctx_sb_gate = [(mm, oproj_last)]

    def groups3_mk():
        groups3 = []
        for s in range(2):
            groups3.append(dict(
                act=T["y16all"], a0=2048 * s, an=2048,
                jobs=[dict(kind="feat", w=T["wk2b"], nstrip=4, out=T["kt2"],
                           wbf16=True, wkey="wk2",
                           ocol0=2048 * s),
                      dict(kind="tok", w=T["wv2b"], nstrip=4, out=T["v2"],
                           wbf16=True, wkey="wv2")]))
        groups3.append(dict(
            act=T["y16own"], a0=0, an=TOWN,
            jobs=[dict(kind="feat", w=T["wq2"], nstrip=4, out=T["qt2"],
                       ocol0=0)]))
        return groups3

    def phase_b(pl, evt):
        sh = dict(d2d=0)
        return [(gen_d2d_b(pl, evt), 1),
                (gen_attn(pl, evt, sh, T["qt1"], T["kt1"], T["v1"],
                          T["wo1b"], True, wob_key="wo1"), 8)]

    run_merged(phase_b)

    def phase_b3(pl, evt):
        return [(gen_proj(pl, evt, groups3_mk(), a1_elems=16384,
                          a1_n=2), 1)]

    run_merged(phase_b3)

    def phase_b4(pl, evt):
        sh = dict(d2d=0)
        return [(gen_d2d_b4(pl, evt), 1),
                (gen_attn(pl, evt, sh, T["qt2"], T["kt2"], T["v2"],
                          T["wo2b"], False, wob_key="wo2"), 8)]

    run_merged(phase_b4)


    def phase_c(pl, evt):
        tiles = [dict(src=T["u32"], i=i, out16=T["z16"], name="z",
                      sb_cols=(V_L2S, V_L2B)) for i in range(4)]
        groups6 = []
        for s in range(2):
            groups6.append(dict(
                act=T["z16"], a0=1024 * s, an=1024,
                gates=[("st16", "z", 2 * s), ("st16", "z", 2 * s + 1)],
                jobs=[dict(kind="feat", w=T["wf1b"], nstrip=16, out=T["h16"],
                           gelu=True, wbf16=True, wkey="wf1",
                           ocol0=1024 * s)]))
        return [(gen_d2d_c(pl, evt), 1), (gen_ln(pl, evt, tiles), 1),
                (gen_proj(pl, evt, groups6, a1_elems=16384, a1_n=1), 6)]

    run_merged(phase_c)
    D2D[0] = S_D2D.n


    def final_phase():
        pl = Plan()
        esl = ExitStack()
        HB = esl.enter_context(nc.sbuf_tensor(uid("hb"), [128, 32768], BF16))
        MB = esl.enter_context(nc.sbuf_tensor(uid("mb"), [128, 2 * 512],
                                               mybir.dt.uint8))
        FA = esl.enter_context(nc.sbuf_tensor(uid("fa"), [128, 16 * 512],
                                               F32))
        FO = esl.enter_context(nc.sbuf_tensor(uid("fo"), [128, 2 * 512], F32))
        ws_gate = [[], []]
        fo_gate = [[], []]
        hb_gate = []
        mb_gate = []
        fa_gate = [[], []]
        slot_gate = [[], []]
        lr_gate = []
        rows_gate = []

        for tt_i in range(4):
            tc = slice(TT * tt_i, TT * (tt_i + 1))
            t_h = pl.op("sync", lambda sy, tc=tc: sy.dma_start(
                HB[:, :].rearrange("p (c t) -> p c t", c=64),
                T["h16"][:, tc].rearrange("(c p) t -> p c t", p=128)),
                waits=hb_gate, inc=S_A1, inck=16)
            pl.op("sync", lambda sy, tt_i=tt_i: sy.dma_start(
                rows(R_L1), T["mrows"][0:1, TT * tt_i:TT * (tt_i + 1)]),
                waits=rows_gate, inc=S_ROW, inck=16)
            t_mr = pl.op("sync", lambda sy, tt_i=tt_i: sy.dma_start(
                rows(R_L2),
                T["mrows"][0:1, TOWN + TT * tt_i:TOWN + TT * (tt_i + 1)]),
                inc=S_ROW, inck=16)

            def fmb(t):
                t.matmul(PSW[3][:, 0:512], ONESR[:, :], rows(R_L1),
                         start=True, stop=True)
                return t.matmul(PSW[3][:, 512:1024], ONESR[:, :], rows(R_L2),
                                start=True, stop=True)
            t_mb = pl.op("tensor", fmb, waits=[(S_ROW, t_mr)] + mb_gate,
                         inc=mm)
            t_mbc = pl.op("scalar", lambda sc: (
                sc.copy(MB[:, 0:512], PSW[3][:, 0:512]),
                sc.copy(MB[:, 512:1024], PSW[3][:, 512:1024]))[-1],
                waits=[(mm, t_mb)] + mb_gate, inc=ex)
            rows_gate = [(mm, t_mb)]
            mb_next = []
            for oc in range(CI):
                wp = (tt_i * CI + oc) % 2
                t_w = pl.op("sync", lambda sy, wp=wp, oc=oc: sy.dma_start(
                    WS[wp][:, :], T["wf2b"][128 * oc:128 * (oc + 1), :]),
                    waits=ws_gate[wp] + [(S_D2D, D2D[0])], inc=S_WS[wp],
                    inck=16)
                si = oc % 2
                sl = PSW[si][:, 0:512]

                def fm(t, wp=wp, sl=sl):
                    for c in range(64):
                        ins = t.matmul(sl, WS[wp][:, 128 * c:128 * (c + 1)],
                                       HB[:, 512 * c:512 * (c + 1)],
                                       start=(c == 0), stop=(c == 63))
                    return ins
                t_mm2 = pl.op("tensor", fm,
                              waits=[(S_WS[wp], t_w), (S_A1, t_h)] +
                              slot_gate[si], inc=mm)
                ws_gate[wp] = [(mm, t_mm2)]
                slr = PSW[2][:, 0:512]

                def flr(t, oc=oc, tc=tc):
                    return t.matmul(slr, WUP[:, 128 * oc:128 * (oc + 1)],
                                    T1[:, tc], start=True, stop=True)
                t_lr = pl.op("tensor", flr, waits=lr_gate, inc=mm)
                orow = slice(128 * oc, 128 * (oc + 1))
                fop = (tt_i * CI + oc) % 2
                B0 = 4096 * fop

                def fld(sy, orow=orow, tc=tc, B0=B0):
                    sy.dma_start(FA[:, B0:B0 + 512],
                                 T["xn32own"][orow, tc]).then_inc(S_FL.h, 16)
                    sy.dma_start(FA[:, B0 + 512:B0 + 1024],
                                 T["u32"][orow, tc]).then_inc(S_FL.h, 16)
                    sy.dma_start(FA[:, B0 + 1024:B0 + 1536],
                                 T["xto"][orow, tc]).then_inc(S_FL.h, 16)
                    return sy.dma_start(FA[:, B0 + 1536:B0 + 2048],
                                        T["s32"][orow, tc])
                S_FL.n += 48
                t_fl = pl.op("sync", fld, waits=fa_gate[fop], inc=S_FL,
                             inck=16)

                def fev(v, sl=sl, oc=oc, B0=B0):
                    v.tensor_add(FA[:, B0 + 2048:B0 + 2560], sl,
                                 FA[:, B0 + 512:B0 + 1024])
                    v.tensor_add(FA[:, B0 + 2560:B0 + 3072], slr,
                                 FA[:, B0:B0 + 512])
                    v.tensor_scalar(FA[:, B0 + 3072:B0 + 3584],
                                    FA[:, B0:B0 + 512],
                                    VECS[:, V_SB * 16 + oc:V_SB * 16 + oc + 1],
                                    None, ALU.add)
                    v.select(FA[:, B0 + 3584:B0 + 4096], MB[:, 512:1024],
                             FA[:, B0 + 2560:B0 + 3072],
                             FA[:, B0 + 2048:B0 + 2560])
                    return v.select(FA[:, B0 + 2048:B0 + 2560], MB[:, 0:512],
                                    FA[:, B0 + 3072:B0 + 3584],
                                    FA[:, B0 + 3584:B0 + 4096])
                t_ev = pl.op("vector", fev,
                             waits=[(mm, t_mm2), (mm, t_lr), (S_FL, t_fl),
                                    (ex, t_mbc)], inc=ve)
                slot_gate[si] = [(ve, t_ev)]
                lr_gate = [(ve, t_ev)]
                mb_next = [(ve, t_ev)]
                t_g = pl.op("scalar", lambda sc, B0=B0: sc.activation(
                    FA[:, B0 + 3584:B0 + 4096],
                    FA[:, B0 + 2048:B0 + 2560], AF.Silu),
                    waits=[(ve, t_ev)], inc=ex)

                def ffin(v, fop=fop, B0=B0):
                    v.tensor_mul(FA[:, B0 + 2048:B0 + 2560],
                                 FA[:, B0 + 3584:B0 + 4096],
                                 FA[:, B0 + 1536:B0 + 2048])
                    return v.tensor_add(FO[:, 512 * fop:512 * (fop + 1)],
                                        FA[:, B0 + 2048:B0 + 2560],
                                        FA[:, B0 + 1024:B0 + 1536])
                t_f = pl.op("vector", ffin,
                            waits=[(ex, t_g)] + fo_gate[fop], inc=ve)
                fa_gate[fop] = [(ve, t_f)]
                t_st = pl.op("scalar", lambda sc, fop=fop, orow=orow, tc=tc:
                             sc.dma_start(T["outT"][orow, tc],
                                          FO[:, 512 * fop:512 * (fop + 1)]),
                             waits=[(ve, t_f)], inc=S_EV[fop], inck=16)
                fo_gate[fop] = [(S_EV[fop], t_st)]
            hb_gate = [(mm, mm.n)]
            mb_gate = mb_next
        pl.op("sync", lambda sy: sy.nop(), waits=drain_waits())
        pl.run(nc)
        esl.close()

    final_phase()
    return nc, es


def _block_w(wt, nstrip, wo):
    din, dout = wt.shape
    ci = din // 128
    a = wt.reshape(ci, 128, nstrip, wo).transpose(2, 1, 0, 3)
    return np.ascontiguousarray(a.reshape(nstrip * 128, ci * wo))


def _pack_vec(v):
    return np.ascontiguousarray(v.reshape(16, 128).T)


def kernel(L0_attn_hidden_states, L0_attn_w_qkv, L0_attn_w_o,
           L0_ffn_simple_bias, L0_ffn_w_down, L0_ffn_w_up, L0_ffn_w_router,
           L0_ffn_ln1_scale, L0_ffn_ln1_bias, L0_ffn_wq, L0_ffn_wk,
           L0_ffn_wv, L0_ffn_wo, L0_ffn_ln2_scale, L0_ffn_ln2_bias,
           L0_ffn_w_ff1, L0_ffn_w_ff2):
    f32 = lambda a: np.asarray(a, dtype=np.float32)
    x = f32(L0_attn_hidden_states)
    qkv = f32(L0_attn_w_qkv)

    wq1 = _block_w(np.ascontiguousarray(qkv[0:D].T), 4, 512)
    wk1 = _block_w(np.ascontiguousarray(qkv[D:2 * D].T), 4, 512)
    wv1 = _block_w(np.ascontiguousarray(qkv[2 * D:3 * D].T), 4, 512)
    wo1 = _block_w(np.ascontiguousarray(f32(L0_attn_w_o).T), 4, 512)
    wq2 = _block_w(np.ascontiguousarray(f32(L0_ffn_wq).T), 4, 512)
    wk2 = _block_w(np.ascontiguousarray(f32(L0_ffn_wk).T), 4, 512)
    wv2 = _block_w(np.ascontiguousarray(f32(L0_ffn_wv).T), 4, 512)
    wo2 = _block_w(np.ascontiguousarray(f32(L0_ffn_wo).T), 4, 512)
    wf1 = _block_w(np.ascontiguousarray(f32(L0_ffn_w_ff1).T), 16, 512)
    wf2 = _block_w(np.ascontiguousarray(f32(L0_ffn_w_ff2).T), 16, 128)

    vecs = np.stack([_pack_vec(f32(v)) for v in
                     [L0_ffn_simple_bias, L0_ffn_ln1_scale, L0_ffn_ln1_bias,
                      L0_ffn_ln2_scale, L0_ffn_ln2_bias]], axis=1)
    vecs = np.ascontiguousarray(vecs.reshape(128, 80))
    wrt = np.ascontiguousarray(
        f32(L0_ffn_w_router).T.reshape(16, 128, 3).transpose(1, 0, 2)
        .reshape(128, 48))
    wdt = np.ascontiguousarray(
        f32(L0_ffn_w_down).T.reshape(16, 128, RK).transpose(1, 0, 2)
        .reshape(128, 16 * RK))
    wupt = np.ascontiguousarray(f32(L0_ffn_w_up).T)

    cmasks = []
    for par in range(2):
        p = np.arange(128)[:, None]
        f = np.arange(512)[None, :]
        m = np.empty((128, 8 * 512), dtype=np.float32)
        for d in range(8):
            allowed = (128 * d + p) <= (128 * (2 * (f // 128) + par) + f % 128)
            m[:, 512 * d:512 * (d + 1)] = np.where(allowed, 0.0, NEG)
        cmasks.append(m.astype(ml_dtypes.bfloat16))

    nc, es = build_program()
    shared = dict(vecs=vecs, wrt=wrt, wdt=wdt, wupt=wupt,
                  wq1=wq1, wk1=wk1, wv1=wv1, wo1=wo1, wq2=wq2, wk2=wk2,
                  wv2=wv2, wo2=wo2, wf1=wf1, wf2=wf2)
    in_maps = []
    for c in range(8):
        b, par = c // 2, c % 2
        gsl = np.concatenate(
            [np.arange(128 * (2 * L + par), 128 * (2 * L + par) + 128)
             for L in range(16)])
        m = dict(shared)
        m["xt"] = np.ascontiguousarray(x[b].T)
        m["xto"] = np.ascontiguousarray(x[b][gsl].T)
        m["cmask"] = cmasks[par]
        in_maps.append(m)

    res = run_bass_kernel_spmd(nc, in_maps, core_ids=list(range(8)))
    es.close()

    out = np.empty((B, S, D), dtype=np.float32)
    for c in range(8):
        b, par = c // 2, c % 2
        ot = res.results[c]["outT"]
        for L in range(16):
            g = 2 * L + par
            out[b, 128 * g:128 * (g + 1), :] = \
                ot[:, 128 * L:128 * (L + 1)].T
    return out


if __name__ == "__main__":
    ncx, esx = build_program()
    print("built ok")
    esx.close()


# version 24
# speedup vs baseline: 1.0242x; 1.0242x over previous
import numpy as np
import ml_dtypes
from contextlib import ExitStack

import concourse.bass as bass
import concourse.mybir as mybir
from concourse.bass_utils import run_bass_kernel_spmd

F32 = mybir.dt.float32
BF16 = mybir.dt.bfloat16
AF = mybir.ActivationFunctionType
ALU = mybir.AluOpType

B, S, D, HN, DH, FF, RK = 4, 4096, 2048, 16, 128, 8192, 32
CI = D // 128
TT = 512
TOWN = 2048
SCALE = float(1.0 / np.sqrt(128.0))
INV_D = float(1.0 / D)
EPS = 1e-5
NEG = -1.0e30

V_SB, V_L1S, V_L1B, V_L2S, V_L2B = 0, 1, 2, 3, 4

ENGS = ["sync", "gpsimd", "tensor", "vector", "scalar"]


class Sem:
    def __init__(self, nc, es, name):
        self.h = es.enter_context(nc.semaphore(name))
        self.name = name
        self.n = 0


class Plan:

    def __init__(self):
        self.q = {e: [] for e in ENGS}
        self.memo = {}

    def op(self, ek, build, waits=(), inc=None, inck=1):
        fw = []
        for s, t in waits:
            if not t or t <= 0:
                continue
            key = (ek, s.name)
            if self.memo.get(key, 0) >= t:
                continue
            self.memo[key] = t
            fw.append((s.h, t))
        tag = None
        if inc is not None:
            inc.n += inck
            tag = inc.n

        def fn(eng, fw=fw, build=build, inc=inc, inck=inck):
            for h, t in fw:
                eng.wait_ge(h, t)
            ins = build(eng)
            if inc is not None:
                ins.then_inc(inc.h, inck)

        self.q[ek].append(fn)
        return tag

    def run(self, nc):
        with nc.Block() as blk:
            for e in ENGS:
                fns = self.q[e]
                if fns:
                    getattr(blk, e)(lambda eng, fns=fns: [f(eng) for f in fns])


def drive(pl, gens, evt):
    norm = [(iter(g[0]), g[1]) if isinstance(g, tuple) else (iter(g), 1)
            for g in gens]
    active = [list(x) for x in norm]
    blocked = {}
    progress = True
    while active:
        if not progress:
            raise RuntimeError(
                f"drive deadlock: blocked on {set(blocked.values())}")
        progress = False
        for ent in list(active):
            g, w = ent
            if id(g) in blocked and blocked[id(g)] not in evt:
                continue
            blocked.pop(id(g), None)
            for _ in range(w):
                try:
                    r = next(g)
                    progress = True
                    if isinstance(r, tuple) and r and r[0] == 'need':
                        if r[1] not in evt:
                            blocked[id(g)] = r[1]
                            break
                except StopIteration:
                    active.remove(ent)
                    progress = True
                    break


def build_program(taps=()):
    nc = bass.Bass("TRN2", target_bir_lowering=False)
    es = ExitStack()
    T = {}
    uidc = [0]

    def uid(n):
        uidc[0] += 1
        return f"{n}_{uidc[0]}"

    def dram(name, shape, dtype, kind="Internal"):
        kd = "ExternalOutput" if name in taps else kind
        T[name] = nc.dram_tensor(name, shape, dtype, kind=kd).ap()
        return T[name]

    dram("xt", [D, S], F32, "ExternalInput")
    dram("xto", [D, TOWN], F32, "ExternalInput")
    dram("vecs", [128, 80], F32, "ExternalInput")
    dram("wrt", [128, 48], F32, "ExternalInput")
    dram("wdt", [128, 512], F32, "ExternalInput")
    dram("wupt", [RK, D], F32, "ExternalInput")
    dram("cmask", [128, 8 * 512], BF16, "ExternalInput")
    for nm in ["wq1", "wk1", "wv1", "wo1", "wq2", "wk2", "wv2", "wo2"]:
        dram(nm, [4 * 128, CI * 512], F32, "ExternalInput")
    dram("wf1", [16 * 128, CI * 512], F32, "ExternalInput")
    dram("wf2", [16 * 128, 64 * 128], F32, "ExternalInput")

    dram("xn16all", [D, S], BF16)
    dram("y16all", [D, S], BF16)
    dram("xn32own", [D, TOWN], F32)
    dram("xn16own", [D, TOWN], BF16)
    dram("y16own", [D, TOWN], BF16)
    dram("kt1", [D, S], BF16)
    dram("v1", [S, D], BF16)
    dram("qt1", [D, TOWN], BF16)
    dram("kt2", [D, S], BF16)
    dram("v2", [S, D], BF16)
    dram("qt2", [D, TOWN], BF16)
    dram("s32", [D, TOWN], F32)
    dram("u32", [D, TOWN], F32)
    dram("z16", [D, TOWN], BF16)
    dram("h16", [FF, TOWN], BF16)
    dram("wk1b", [4 * 128, CI * 512], BF16)
    dram("wv1b", [4 * 128, CI * 512], BF16)
    dram("wk2b", [4 * 128, CI * 512], BF16)
    dram("wv2b", [4 * 128, CI * 512], BF16)
    dram("wo1b", [4 * 128, CI * 512], BF16)
    dram("wo2b", [4 * 128, CI * 512], BF16)
    dram("wf1b", [16 * 128, CI * 512], BF16)
    dram("wf2b", [16 * 128, 64 * 128], BF16)
    dram("mrows", [1, 2 * TOWN], F32)
    dram("outT", [D, TOWN], F32, "ExternalOutput")

    semnames = (["dx", "dw", "ds", "mm", "ex", "ve", "ini", "d2d", "a1",
                 "a1b", "row", "st16", "st32", "sty", "fl"] +
                [f"cv{i}" for i in range(5)] +
                [f"xb{i}" for i in range(2)] + [f"ws{i}" for i in range(4)] +
                [f"ev{i}" for i in range(4)] + [f"ld{i}" for i in range(2)] +
                [f"xr{i}" for i in range(4)])
    SM = {s: Sem(nc, es, s) for s in semnames}
    dx, dw, ds, mm, ex, ve = (SM[s] for s in ["dx", "dw", "ds", "mm", "ex", "ve"])
    (S_INI, S_D2D, S_A1, S_A1B, S_ROW, S_ST16, S_ST32, S_STY,
     S_FL) = (SM[s] for s in ["ini", "d2d", "a1", "a1b", "row", "st16",
                              "st32", "sty", "fl"])
    S_CV = [SM[f"cv{i}"] for i in range(5)]
    S_XB = [SM[f"xb{i}"] for i in range(2)]
    S_WS = [SM[f"ws{i}"] for i in range(4)]
    S_EV = [SM[f"ev{i}"] for i in range(4)]
    S_LD = [SM[f"ld{i}"] for i in range(2)]
    S_XR = [SM[f"xr{i}"] for i in range(4)]
    ALLDMA = ([S_INI, S_D2D, S_A1, S_A1B, S_ROW, S_ST16, S_ST32, S_STY,
               S_FL] +
              S_CV + S_XB + S_WS + S_EV + S_LD + S_XR)

    def drain_waits():
        return [(s, s.n) for s in ALLDMA if s.n > 0]


    sb = lambda name, shape, dt: es.enter_context(nc.sbuf_tensor(name, shape, dt))
    WS = [sb("ws0", [128, 8192], BF16), sb("ws1", [128, 8192], BF16)]
    ONESC = sb("onesc", [128, 1], BF16)
    ONESR = sb("onesr", [1, 128], F32)
    EPSC = sb("epsc", [128, 1], F32)
    VECS = sb("vecsb", [128, 80], F32)
    WRT = sb("wrtb", [128, 48], F32)
    WDT = sb("wdtb", [128, 512], BF16)
    WUP = sb("wupb", [RK, D], BF16)
    CMK = sb("cmkb", [128, 8 * 512], BF16)
    IDN = sb("idn", [128, 128], BF16)
    T1 = sb("t1res", [RK, TOWN], BF16)
    ROWS = sb("rows", [1, 3584], F32)
    L3 = sb("l3", [3, 512], F32)
    PSW = [es.enter_context(nc.psum_tensor(f"psw{i}", [128, 1024], F32))
           for i in range(4)]

    R_MEAN, R_T2, R_MSQ, R_VAR, R_MR, R_L1, R_L2 = (
        0, 512, 1024, 1536, 2048, 2560, 3072)
    R_TMP = R_T2

    def rows(off, n=512):
        return ROWS[:, off:off + n]

    def bcast_ap(psw, half):
        return bass.AP(psw, 512 * half, [[1024, 128], [0, CI], [1, 512]])

    def wide_ap(psw):
        return bass.AP(psw, 0, [[1024, 128], [1, 1024]])

    pl = Plan()
    pl.op("gpsimd", lambda g: g.memset(ONESC[:, :], 1.0))
    pl.op("gpsimd", lambda g: g.memset(ONESR[:, :], 1.0))
    pl.op("gpsimd", lambda g: g.memset(EPSC[:, :], EPS))
    pl.op("gpsimd", lambda g: g.memset(IDN[:, :], 1.0))
    pl.op("gpsimd", lambda g: g.affine_select(IDN[:, :], IDN[:, :],
                                              [[1, 128]], ALU.is_equal, 0.0,
                                              base=0, channel_multiplier=-1))
    pl.op("gpsimd", lambda g: g.dma_start(WDT[:, :], T["wdt"]),
          inc=S_INI, inck=16)
    pl.op("gpsimd", lambda g: g.dma_start(WUP[:, :], T["wupt"]),
          inc=S_INI, inck=16)
    pl.op("sync", lambda sy: sy.dma_start(VECS[:, :], T["vecs"]),
          inc=S_INI, inck=16)
    pl.op("sync", lambda sy: sy.dma_start(WRT[:, :], T["wrt"]),
          inc=S_INI, inck=16)
    pl.op("sync", lambda sy: sy.dma_start(CMK[:, :], T["cmask"]),
          inc=S_INI, inck=16)
    pl.op("sync", lambda sy: sy.nop(), waits=[(S_INI, S_INI.n)])
    pl.run(nc)

    def gen_ln(pl, evt, tiles):
        esl = ExitStack()
        XB = esl.enter_context(nc.sbuf_tensor(uid("xb"), [128, 8192], F32))
        XN = esl.enter_context(nc.sbuf_tensor(uid("xnb"), [128, 8192], F32))
        B16 = esl.enter_context(nc.sbuf_tensor(uid("b16"), [128, 16384], BF16))
        evt.setdefault("_esl", []).append(esl)
        x16 = B16[:, 0:8192]
        sq16 = B16[:, 8192:16384]
        x16v = x16.rearrange("p (c t) -> p c t", c=CI)
        sq16v = sq16.rearrange("p (c t) -> p c t", c=CI)
        xnv = XN[:, :].rearrange("p (c t) -> p c t", c=CI)

        xb_gate = []
        xn_gate = []
        x16_gate = []
        sq_gate = []
        rowb_gate = []
        bc_gate = []
        msk_prev = 0

        for job in tiles:
            src, i = job["src"], job["i"]
            out16, out32 = job["out16"], job.get("out32")
            sb_cols = job.get("sb_cols")
            second = job.get("second")
            router = job.get("router", False)
            name = job["name"]
            csl = slice(TT * i, TT * (i + 1))
            t_ld = pl.op("sync", lambda sy, src=src, csl=csl: sy.dma_start(
                XB[:, :].rearrange("p (c t) -> p c t", c=CI),
                src[:, csl].rearrange("(c p) t -> p c t", p=128)),
                waits=xb_gate, inc=S_XB[0], inck=16)
            t_x16 = pl.op("scalar", lambda sc: sc.activation(
                x16, XB[:, :], AF.Copy),
                waits=[(S_XB[0], t_ld)] + x16_gate, inc=ex)
            t_sq = pl.op("scalar", lambda sc: sc.activation(
                sq16, XB[:, :], AF.Square), waits=sq_gate, inc=ex)
            yield

            def fsum(t):
                for c in range(CI):
                    t.matmul(PSW[2][0:1, 0:512], ONESC[:, :], x16v[:, c, :],
                             start=(c == 0), stop=(c == CI - 1))
                for c in range(CI):
                    ins = t.matmul(PSW[2][0:1, 512:1024], ONESC[:, :],
                                   sq16v[:, c, :], start=(c == 0),
                                   stop=(c == CI - 1))
                return ins
            t_sum = pl.op("tensor", fsum,
                          waits=[(ex, t_x16), (ex, t_sq)] + rowb_gate, inc=mm)

            def frows(v):
                v.tensor_scalar(rows(R_MEAN), PSW[2][0:1, 0:512], INV_D, None,
                                ALU.mult)
                v.tensor_scalar(rows(R_T2), PSW[2][0:1, 512:1024], INV_D, None,
                                ALU.mult)
                v.tensor_mul(rows(R_MSQ), rows(R_MEAN), rows(R_MEAN))
                return v.tensor_sub(rows(R_VAR), rows(R_T2), rows(R_MSQ))
            t_rows = pl.op("vector", frows, waits=[(mm, t_sum)], inc=ve)
            rowb_gate = [(ve, t_rows)]

            def frstd(sc):
                sc.activation(rows(R_T2), rows(R_VAR), AF.Ln,
                              bias=EPSC[0:1, 0:1])
                return sc.activation(rows(R_MSQ), rows(R_T2), AF.Exp,
                                     scale=-0.5)
            t_rstd = pl.op("scalar", frstd, waits=[(ve, t_rows)], inc=ex)

            def fmr(v, second=second):
                ins = v.tensor_mul(rows(R_MR), rows(R_MEAN), rows(R_MSQ))
                if second is not None:
                    v.tensor_mul(rows(R_T2), rows(R_MSQ), rows(R_MSQ))
                    ins = v.tensor_mul(rows(R_T2), rows(R_VAR), rows(R_T2))
                return ins
            t_mr = pl.op("vector", fmr, waits=[(ex, t_rstd)], inc=ve)

            def fbc(t):
                t.matmul(PSW[3][:, 0:512], ONESR[:, :], rows(R_MSQ),
                         start=True, stop=True)
                return t.matmul(PSW[3][:, 512:1024], ONESR[:, :], rows(R_MR),
                                start=True, stop=True)
            t_bc = pl.op("tensor", fbc, waits=[(ve, t_mr)] + bc_gate, inc=mm)

            def fxn(v):
                xbv = XB[:, :].rearrange("p (c t) -> p c t", c=CI)
                v.tensor_mul(xnv, xbv, bcast_ap(PSW[3], 0))
                return v.tensor_sub(xnv, xnv, bcast_ap(PSW[3], 1))
            t_xn = pl.op("vector", fxn, waits=[(mm, t_bc)] + xn_gate, inc=ve)
            bc_gate = [(ve, t_xn)]
            yield

            if sb_cols is None:
                t_o16 = pl.op("scalar", lambda sc: sc.activation(
                    x16, XN[:, :], AF.Copy),
                    waits=[(ve, t_xn), (mm, t_sum)], inc=ex)
                o16_sem = ex
            else:
                scol, bcol = sb_cols
                def fo16(v, scol=scol, bcol=bcol):
                    for c in range(CI):
                        ins = v.tensor_scalar(
                            x16[:, TT * c:TT * (c + 1)],
                            XN[:, TT * c:TT * (c + 1)],
                            VECS[:, scol * 16 + c:scol * 16 + c + 1],
                            VECS[:, bcol * 16 + c:bcol * 16 + c + 1],
                            ALU.mult, ALU.add)
                    return ins
                t_o16 = pl.op("vector", fo16,
                              waits=[(ve, t_xn), (mm, t_sum)], inc=ve)
                o16_sem = ve
            t_st16 = pl.op("scalar", lambda sc, out16=out16, csl=csl:
                           sc.dma_start(
                out16[:, csl].rearrange("(c p) t -> p c t", p=128), x16v),
                waits=[(o16_sem, t_o16)], inc=S_ST16, inck=16)
            x16_gate = [(S_ST16, t_st16)]
            xn_gate = [(o16_sem, t_o16)]
            evt[("st16", name, i)] = (S_ST16, t_st16)
            if out32 is not None:
                t_st32 = pl.op("scalar", lambda sc, out32=out32, csl=csl:
                               sc.dma_start(
                    out32[:, csl].rearrange("(c p) t -> p c t", p=128), xnv),
                    waits=[(ve, t_xn)], inc=S_ST32, inck=16)
                xn_gate.append((S_ST32, t_st32))
                evt[("st32", name, i)] = (S_ST32, t_st32)
            yield

            if router:
                def frt(t):
                    for c in range(CI):
                        ins = t.matmul(PSW[2][0:3, 0:512],
                                       WRT[:, 3 * c:3 * (c + 1)],
                                       XN[:, TT * c:TT * (c + 1)],
                                       start=(c == 0), stop=(c == CI - 1))
                    return ins
                t_rt = pl.op("tensor", frt,
                             waits=[(ve, t_xn), (ve, t_rows)], inc=mm)
                xn_gate.append((mm, t_rt))
                t_rl = pl.op("scalar", lambda sc: sc.copy(
                    L3[:, :], PSW[2][0:3, 0:512]), waits=[(mm, t_rt)], inc=ex)
                pl.op("sync", lambda sy: sy.dma_start(rows(R_L1), L3[1:2, :]),
                      waits=[(ex, t_rl), (ve, msk_prev)], inc=S_ROW, inck=16)
                t_rmv = pl.op("sync", lambda sy: sy.dma_start(
                    rows(R_L2), L3[2:3, :]), inc=S_ROW, inck=16)

                def fmask(v):
                    v.tensor_tensor(rows(R_MEAN), L3[0:1, :], rows(R_L1),
                                    ALU.is_ge)
                    v.tensor_tensor(rows(R_VAR), L3[0:1, :], rows(R_L2),
                                    ALU.is_ge)
                    v.tensor_mul(rows(R_MEAN), rows(R_MEAN), rows(R_VAR))
                    v.tensor_tensor(rows(R_VAR), rows(R_L1), rows(R_L2),
                                    ALU.is_ge)
                    v.tensor_scalar(rows(R_MSQ), rows(R_MEAN), -1.0, 1.0,
                                    ALU.mult, ALU.add)
                    return v.tensor_mul(rows(R_VAR), rows(R_MSQ), rows(R_VAR))
                t_msk = pl.op("vector", fmask,
                              waits=[(S_ROW, t_rmv), (ex, t_rl)], inc=ve)
                t_ms = pl.op("scalar", lambda sc, i=i: sc.dma_start(
                    T["mrows"][0:1, TT * i:TT * (i + 1)], rows(R_MEAN)),
                    waits=[(ve, t_msk)], inc=S_ROW, inck=16)
                t_ms = pl.op("scalar", lambda sc, i=i: sc.dma_start(
                    T["mrows"][0:1, TOWN + TT * i:TOWN + TT * (i + 1)],
                    rows(R_VAR)), inc=S_ROW, inck=16)
                evt[("mrow", i)] = (S_ROW, t_ms)
                msk_prev = t_msk
                rowb_gate = [(ve, t_msk), (ex, t_rl)]
                yield

            if second is not None:
                def frstd2(sc):
                    sc.activation(rows(R_T2), rows(R_T2), AF.Ln,
                                  bias=EPSC[0:1, 0:1])
                    return sc.activation(rows(R_T2), rows(R_T2), AF.Exp,
                                         scale=-0.5)
                t_rstd2 = pl.op("scalar", frstd2, waits=[(ve, t_mr)], inc=ex)
                t_bc2 = pl.op("tensor", lambda t: t.matmul(
                    PSW[3][:, 0:512], ONESR[:, :], rows(R_T2),
                    start=True, stop=True),
                    waits=[(ex, t_rstd2), (ve, t_xn)], inc=mm)
                def fy(v):
                    return v.tensor_mul(
                        XB[:, :].rearrange("p (c t) -> p c t", c=CI),
                        xnv, bcast_ap(PSW[3], 0))
                t_y = pl.op("vector", fy, waits=[(mm, t_bc2)], inc=ve)
                bc_gate = [(ve, t_y)]
                scol, bcol = second["scale_col"], second["bias_col"]

                def fy16(v, scol=scol, bcol=bcol):
                    for c in range(CI):
                        ins = v.tensor_scalar(
                            sq16[:, TT * c:TT * (c + 1)],
                            XB[:, TT * c:TT * (c + 1)],
                            VECS[:, scol * 16 + c:scol * 16 + c + 1],
                            VECS[:, bcol * 16 + c:bcol * 16 + c + 1],
                            ALU.mult, ALU.add)
                    return ins
                t_y16 = pl.op("vector", fy16,
                              waits=[(ve, t_y), (mm, t_sum)], inc=ve)
                t_sty = pl.op("scalar", lambda sc, second=second, csl=csl:
                              sc.dma_start(
                    second["out"][:, csl].rearrange("(c p) t -> p c t", p=128),
                    sq16v), waits=[(ve, t_y16)], inc=S_STY, inck=16)
                evt[("sty", second["name"], i)] = (S_STY, t_sty)
                xb_gate = [(ve, t_y16)]
                sq_gate = [(S_STY, t_sty)]
            else:
                xb_gate = [(ve, t_xn), (ex, t_sq)]
                sq_gate = [(mm, t_sum)]
            yield
        evt[("ln_done", id(tiles))] = True

    def gen_proj(pl, evt, groups, a1_elems=32768, slots=(0, 1, 2, 3),
                 ev_sems=None, a1_sem=None, sh_ws=None, ws_bufs=None,
                 coarse=False, a1_n=2):
        if ev_sems is None:
            ev_sems = S_EV
        if a1_sem is None:
            a1_sem = S_A1
        if ws_bufs is None:
            ws_bufs = [(WS[0], S_WS[0]), (WS[1], S_WS[1])]
        sh_ws = dict(ws_gate=[[] for _ in ws_bufs], wseq=[0])
        nslot = len(slots)
        nev = len(ev_sems)
        esl = ExitStack()
        A1B_ = [esl.enter_context(nc.sbuf_tensor(uid("a1"), [128, a1_elems],
                                                 BF16)) for _ in range(a1_n)]
        A1S_ = ([a1_sem] if a1_n == 1 else [S_A1, S_A1B])[:a1_n]
        EV = esl.enter_context(nc.sbuf_tensor(uid("ev"), [128, nev * 512],
                                              BF16))
        evt.setdefault("_esl", []).append(esl)
        seg_tok = a1_elems // CI
        ws_gate = sh_ws["ws_gate"]
        ev_gate = [[] for _ in range(nev)]
        slot_gate = [[] for _ in range(nslot)]
        a1_gate = [[] for _ in range(a1_n)]
        a1_last = [0] * a1_n
        aseq = [0]
        sseq = [0]
        eseq = [0]
        wseq = sh_ws["wseq"]

        def psum_slot(k):
            i = slots[k]
            return PSW[i // 2][:, 512 * (i % 2):512 * (i % 2 + 1)]

        def evac_store(t_mm, make_store, gelu=False):
            ei = eseq[0] % nev
            eseq[0] += 1
            sl_i = (sseq[0] - 1) % nslot
            sl = psum_slot(sl_i)
            if gelu or ei % 2:
                t_ev = pl.op("scalar", lambda sc, sl=sl, ei=ei: sc.activation(
                    EV[:, 512 * ei:512 * (ei + 1)], sl,
                    AF.Gelu_apprx_tanh if gelu else AF.Copy),
                    waits=[(mm, t_mm)] + ev_gate[ei], inc=ex)
                esem = ex
            else:
                t_ev = pl.op("vector", lambda v, sl=sl, ei=ei: v.tensor_copy(
                    EV[:, 512 * ei:512 * (ei + 1)], sl),
                    waits=[(mm, t_mm)] + ev_gate[ei], inc=ve)
                esem = ve
            slot_gate[sl_i] = [(esem, t_ev)]
            t_st = pl.op("scalar", lambda sc, ei=ei: make_store(
                sc, EV[:, 512 * ei:512 * (ei + 1)]),
                waits=[(esem, t_ev)], inc=ev_sems[ei], inck=16)
            ev_gate[ei] = [(ev_sems[ei], t_st)]
            return t_st

        for grp in groups:
            act, a0, an = grp["act"], grp["a0"], grp["an"]
            for key in grp.get("gates", ()):
                while key not in evt:
                    yield ('need', key)
            ggates = [evt[k] for k in grp.get("gates", ())]
            for seg in range((an + seg_tok - 1) // seg_tok):
                ai = aseq[0] % a1_n
                aseq[0] += 1
                A1 = A1B_[ai]
                asem = A1S_[ai]
                sa0 = a0 + seg_tok * seg
                sn = min(seg_tok, a0 + an - sa0)
                nct = sn // TT
                t_a1 = pl.op("sync", lambda sy, act=act, sa0=sa0, sn=sn,
                             A1=A1: sy.dma_start(
                    A1[:, 0:CI * sn].rearrange("p (c t) -> p c t", c=CI),
                    act[:, sa0:sa0 + sn].rearrange("(c p) t -> p c t", p=128)),
                    waits=a1_gate[ai] + ggates, inc=asem, inck=16)
                for job in grp["jobs"]:
                    if job.get("wkey"):
                        while job["wkey"] not in CVT:
                            yield ('need', ("cvt", job["wkey"]))
                        job["wgate"] = [CVT[job["wkey"]]]
                    if job["kind"] == "rank":
                        for tt_i in range(nct):
                            sl_i = sseq[0] % nslot
                            sl = psum_slot(sl_i)
                            sseq[0] += 1

                            def fr(t, tt_i=tt_i, sl=sl, sn=sn, A1=A1):
                                for c in range(CI):
                                    ins = t.matmul(
                                        sl[0:RK, :],
                                        WDT[:, RK * c:RK * (c + 1)],
                                        A1[:, c * sn + TT * tt_i:
                                           c * sn + TT * (tt_i + 1)],
                                        start=(c == 0), stop=(c == CI - 1))
                                return ins
                            t_mm = pl.op("tensor", fr,
                                         waits=[(asem, t_a1)] +
                                         slot_gate[sl_i], inc=mm)
                            a1_last[ai] = t_mm
                            t_ev = pl.op("vector",
                                         lambda v, sl=sl, tt_i=tt_i, sa0=sa0,
                                         a0=a0: v.tensor_copy(
                                T1[:, sa0 - a0 + TT * tt_i:
                                   sa0 - a0 + TT * (tt_i + 1)],
                                sl[0:RK, :]), waits=[(mm, t_mm)], inc=ve)
                            slot_gate[sl_i] = [(ve, t_ev)]
                            yield
                        continue
                    for st in range(job["nstrip"]):
                        wp = wseq[0] % len(ws_bufs)
                        wseq[0] += 1
                        wbuf, wsem = ws_bufs[wp]
                        wdram = job["w"]
                        if job.get("wbf16"):
                            t_w = pl.op("sync",
                                        lambda sy, wbuf=wbuf, st=st,
                                        wdram=wdram: sy.dma_start(
                                wbuf[:, :],
                                wdram[128 * st:128 * (st + 1), :]),
                                waits=ws_gate[wp] + job.get("wgate", []),
                                inc=wsem, inck=16)
                        else:
                            t_w = pl.op("gpsimd",
                                        lambda g, wbuf=wbuf, st=st,
                                        wdram=wdram: g.dma_start(
                                wbuf[:, :],
                                wdram[128 * st:128 * (st + 1), :]),
                                waits=ws_gate[wp], inc=wsem, inck=16)
                        last_mm = 0
                        if job["kind"] == "feat":
                            for tt_i in range(nct):
                                for oc in range(4):
                                    sl_i = sseq[0] % nslot
                                    sl = psum_slot(sl_i)
                                    sseq[0] += 1

                                    def fm(t, wbuf=wbuf, tt_i=tt_i, oc=oc,
                                           sl=sl, sn=sn, A1=A1):
                                        for c in range(CI):
                                            ins = t.matmul(
                                                sl,
                                                wbuf[:, 512 * c + 128 * oc:
                                                     512 * c +
                                                     128 * (oc + 1)],
                                                A1[:, c * sn + TT * tt_i:
                                                   c * sn + TT * (tt_i + 1)],
                                                start=(c == 0),
                                                stop=(c == CI - 1))
                                        return ins
                                    t_mm = pl.op("tensor", fm,
                                                 waits=[(wsem, t_w),
                                                        (asem, t_a1)] +
                                                 slot_gate[sl_i], inc=mm)
                                    last_mm = t_mm
                                    a1_last[ai] = t_mm
                                    ws_gate[wp] = [(mm, t_mm)]
                                    orow = 128 * (4 * st + oc)
                                    ocol = sa0 - a0 + TT * tt_i + \
                                        job.get("ocol0", a0)
                                    out = job["out"]
                                    evac_store(
                                        t_mm,
                                        lambda sc, ev, orow=orow, ocol=ocol,
                                        out=out: sc.dma_start(
                                            out[orow:orow + 128,
                                                ocol:ocol + TT], ev),
                                        gelu=job.get("gelu", False))
                                    if not coarse:
                                        yield
                        else:
                            for tt8 in range(sn // 128):
                                sl_i = sseq[0] % nslot
                                sl = psum_slot(sl_i)
                                sseq[0] += 1

                                def fm(t, wbuf=wbuf, tt8=tt8, sl=sl, sn=sn,
                                       A1=A1):
                                    for c in range(CI):
                                        ins = t.matmul(
                                            sl,
                                            A1[:, c * sn + 128 * tt8:
                                               c * sn + 128 * (tt8 + 1)],
                                            wbuf[:, 512 * c:512 * (c + 1)],
                                            start=(c == 0),
                                            stop=(c == CI - 1))
                                    return ins
                                t_mm = pl.op("tensor", fm,
                                             waits=[(wsem, t_w),
                                                    (asem, t_a1)] +
                                             slot_gate[sl_i], inc=mm)
                                last_mm = t_mm
                                a1_last[ai] = t_mm
                                ws_gate[wp] = [(mm, t_mm)]
                                trow = sa0 + 128 * tt8
                                out = job["out"]
                                evac_store(
                                    t_mm,
                                    lambda sc, ev, trow=trow, st=st, out=out:
                                    sc.dma_start(
                                        out[trow:trow + 128,
                                            512 * st:512 * (st + 1)], ev))
                                if not coarse:
                                    yield
                        ws_gate[wp] = [(mm, last_mm)]
                        if coarse:
                            yield
                a1_gate[ai] = [(mm, a1_last[ai])]

    def run_merged(gens_f):
        pl = Plan()
        evt = {}
        EVT_G[0] = evt
        drive(pl, gens_f(pl, evt), evt)
        pl.op("sync", lambda sy: sy.nop(), waits=drain_waits())
        pl.run(nc)
        for esl in reversed(evt.get("_esl", [])):
            esl.close()

    CVT = {}
    EVT_G = [{}]

    def cv_mk(pl, name, dst, srcn, sem, pieces=2):
        rows_ = {"wk1b": 512, "wv1b": 512, "wk2b": 512, "wv2b": 512,
                 "wo1b": 512, "wo2b": 512, "wf1b": 2048, "wf2b": 2048}[dst]
        step = rows_ // pieces
        t = None
        for i in range(pieces):
            t = pl.op("gpsimd", lambda g, i=i, dst=dst, srcn=srcn,
                      step=step: g.dma_start(
                T[dst][step * i:step * (i + 1), :],
                T[srcn][step * i:step * (i + 1), :]), inc=sem, inck=16)
            yield
        CVT[name] = (sem, t)
        EVT_G[0][("cvt", name)] = True

    def gen_d2d(pl, evt):
        def cv(name, dst, srcn, sem, pieces=2):
            n = T[dst].shape[0] if False else None
            rows_ = {"wk1b": 512, "wv1b": 512, "wk2b": 512, "wv2b": 512,
                     "wo1b": 512, "wo2b": 512, "wf1b": 2048, "wf2b": 2048}[dst]
            step = rows_ // pieces
            t = None
            for i in range(pieces):
                t = pl.op("gpsimd", lambda g, i=i, dst=dst, srcn=srcn,
                          step=step: g.dma_start(
                    T[dst][step * i:step * (i + 1), :],
                    T[srcn][step * i:step * (i + 1), :]), inc=sem, inck=16)
                yield
            CVT[name] = (sem, t)
            EVT_G[0][("cvt", name)] = True
        yield from cv("wk1", "wk1b", "wk1", S_CV[0])
        yield from cv("wv1", "wv1b", "wv1", S_CV[1])

    def gen_d2d_b(pl, evt):
        yield from cv_mk(pl, "wk2", "wk2b", "wk2", S_CV[2])
        yield from cv_mk(pl, "wv2", "wv2b", "wv2", S_CV[3])
        yield from cv_mk(pl, "wo1", "wo1b", "wo1", S_D2D)

    def gen_d2d_b4(pl, evt):
        yield from cv_mk(pl, "wo2", "wo2b", "wo2", S_D2D)
        yield from cv_mk(pl, "wf1", "wf1b", "wf1", S_CV[4], pieces=8)

    def gen_d2d_c(pl, evt):
        yield from cv_mk(pl, "wf2", "wf2b", "wf2", S_D2D, pieces=8)

    def phase_a(pl, evt):
        sec_a = dict(out=T["y16all"], scale_col=V_L1S, bias_col=V_L1B,
                     name="ya")
        sec_o = dict(out=T["y16own"], scale_col=V_L1S, bias_col=V_L1B,
                     name="yo")
        tiles = ([dict(src=T["xt"], i=i, out16=T["xn16all"], name="xa",
                       second=sec_a) for i in range(8)] +
                 [dict(src=T["xto"], i=i, out16=T["xn16own"], name="xo",
                       out32=T["xn32own"], router=True, second=sec_o)
                  for i in range(4)])
        groups = []
        for s in range(4):
            groups.append(dict(
                act=T["xn16all"], a0=1024 * s, an=1024,
                gates=[("st16", "xa", 2 * s), ("st16", "xa", 2 * s + 1)],
                jobs=[dict(kind="feat", w=T["wk1b"], nstrip=4, out=T["kt1"],
                           wbf16=True, wkey="wk1", ocol0=1024 * s),
                      dict(kind="tok", w=T["wv1b"], nstrip=4, out=T["v1"],
                           wbf16=True, wkey="wv1")]))
        groups.append(dict(
            act=T["xn16own"], a0=0, an=TOWN,
            gates=[("st16", "xo", i) for i in range(4)],
            jobs=[dict(kind="feat", w=T["wq1"], nstrip=4, out=T["qt1"],
                       ocol0=0),
                  dict(kind="rank")]))
        return [(gen_ln(pl, evt, tiles), 1),
                (gen_proj(pl, evt, groups, a1_elems=16384, a1_n=1), 6),
                (gen_d2d(pl, evt), 1)]

    run_merged(phase_a)
    D2D = [S_D2D.n]


    def gen_attn(pl, evt, sh, qt, kt, vt, wob, out_s32, wob_key=None,
                 ws_bufs=None, coarse=False):
        esl = ExitStack()
        KB = esl.enter_context(nc.sbuf_tensor(uid("kb"), [128, 8192], BF16))
        VB = esl.enter_context(nc.sbuf_tensor(uid("vb"), [128, 8192], BF16))
        QB = esl.enter_context(nc.sbuf_tensor(uid("qb"), [128, 1024], BF16))
        PT = [esl.enter_context(nc.sbuf_tensor(uid("pt"), [128, 1024], BF16))
              for i in range(3)]
        CTX = esl.enter_context(nc.sbuf_tensor(uid("ctx"), [128, 8192], BF16))
        BCS = esl.enter_context(nc.sbuf_tensor(uid("bcs"), [128, 1024], F32))
        EVF = esl.enter_context(nc.sbuf_tensor(uid("evf"), [128, 1024], F32))
        XR = (esl.enter_context(nc.sbuf_tensor(uid("xr"), [128, 2048], F32))
              if not out_s32 else None)
        evt.setdefault("_esl", []).append(esl)
        if ws_bufs is None:
            ws_bufs = [(WS[0], S_WS[0]), (WS[1], S_WS[1])]
        ws_gate = [[] for _ in ws_bufs]
        wseq = [0]

        bank = {i: [] for i in range(4)}
        kqv_gate = []
        pt_gate = [[], [], []]
        ctxb_gate = [[], []]
        den_gate = []
        evf_gate = [[], []]
        xr_gate = [[], [], [], []]
        bcs_gate = []
        ctx_sb_gate = []

        for tk_i in range(4):
            nk = 8 * (tk_i + 1)
            last_cm = 0
            for pr in range(8):
                h0, h1 = 2 * pr, 2 * pr + 1

                def fk(sy, h0=h0, h1=h1, nk=nk, tk_i=tk_i):
                    sy.dma_start(KB[:, 0:128 * nk],
                                 kt[128 * h0:128 * (h0 + 1), 0:128 * nk]
                                 ).then_inc(S_LD[0].h, 16)
                    sy.dma_start(KB[:, 4096:4096 + 128 * nk],
                                 kt[128 * h1:128 * (h1 + 1), 0:128 * nk]
                                 ).then_inc(S_LD[0].h, 16)
                    sy.dma_start(QB[:, 0:512],
                                 qt[128 * h0:128 * (h0 + 1),
                                    TT * tk_i:TT * (tk_i + 1)]
                                 ).then_inc(S_LD[0].h, 16)
                    return sy.dma_start(QB[:, 512:1024],
                                        qt[128 * h1:128 * (h1 + 1),
                                           TT * tk_i:TT * (tk_i + 1)])
                S_LD[0].n += 48
                t_k = pl.op("sync", fk, waits=kqv_gate, inc=S_LD[0], inck=16)
                t_v = pl.op("sync", lambda sy, pr=pr, nk=nk: sy.dma_start(
                    VB[:, 0:256 * nk].rearrange("p (j f) -> p j f", j=nk),
                    vt[0:128 * nk, 256 * pr:256 * (pr + 1)].rearrange(
                        "(j p) f -> p j f", p=128)), inc=S_LD[0], inck=16)

                t_exp = [0] * nk
                t_pv = [0] * nk
                t_s = [0] * nk
                for j in range(nk + 1):
                    if j < nk:
                        sp = j % 2
                        d = j - (nk - 8)

                        def fs(t, j=j, sp=sp, d=d):
                            if d >= 0:
                                t.matmul(PSW[sp][:, 0:512], IDN[:, :],
                                         CMK[:, 512 * d:512 * (d + 1)],
                                         start=True, stop=False)
                                t.matmul(PSW[sp][:, 512:1024], IDN[:, :],
                                         CMK[:, 512 * d:512 * (d + 1)],
                                         start=True, stop=False)
                            t.matmul(PSW[sp][:, 0:512],
                                     KB[:, 128 * j:128 * (j + 1)],
                                     QB[:, 0:512], start=(d < 0), stop=True)
                            return t.matmul(
                                PSW[sp][:, 512:1024],
                                KB[:, 4096 + 128 * j:4096 + 128 * (j + 1)],
                                QB[:, 512:1024], start=(d < 0), stop=True)
                        t_s[j] = pl.op("tensor", fs,
                                       waits=[(S_LD[0], t_k)] +
                                       bank[2 * sp] + bank[2 * sp + 1],
                                       inc=mm)
                        ewait = [(mm, t_s[j])]
                        jp = j % 3
                        t_exp[j] = pl.op("scalar", lambda sc, jp=jp, sp=sp:
                                         sc.activation(PT[jp][:, :],
                                                       PSW[sp][:, :], AF.Exp,
                                                       scale=SCALE),
                                         waits=ewait + pt_gate[jp], inc=ex)
                        bank[2 * sp] = bank[2 * sp + 1] = [(ex, t_exp[j])]
                    jj = j - 1
                    if jj < 0:
                        if not coarse:
                            yield
                        continue
                    jp = jj % 3

                    def fpv(t, jj=jj, jp=jp, nk=nk):
                        t.matmul(PSW[2][:, 0:512],
                                 VB[:, 256 * jj:256 * jj + 128],
                                 PT[jp][:, 0:512], start=(jj == 0),
                                 stop=(jj == nk - 1))
                        t.matmul(PSW[2][:, 512:1024],
                                 VB[:, 256 * jj + 128:256 * (jj + 1)],
                                 PT[jp][:, 512:1024], start=(jj == 0),
                                 stop=(jj == nk - 1))
                        t.matmul(PSW[3][0:1, 0:512], ONESC[:, :],
                                 PT[jp][:, 0:512], start=(jj == 0),
                                 stop=(jj == nk - 1), skip_group_check=True)
                        return t.matmul(PSW[3][32:33, 0:512],
                                        ONESC[:, :], PT[jp][:, 512:1024],
                                        start=(jj == 0), stop=(jj == nk - 1),
                                        tile_position=(0, 32),
                                        skip_group_check=True)
                    w0 = [(ex, t_exp[jj]), (S_LD[0], t_v)]
                    if jj == 0:
                        w0 += ctxb_gate[0] + ctxb_gate[1] + den_gate
                    t_pv[jj] = pl.op("tensor", fpv, waits=w0, inc=mm)
                    pt_gate[jp] = [(mm, t_pv[jj])]
                    if not coarse:
                        yield
                kqv_gate = [(mm, t_pv[nk - 1])]

                def frc(sc):
                    sc.activation(rows(R_L1), PSW[3][0:1, 0:512], AF.Ln)
                    sc.activation(rows(R_L2), PSW[3][32:33, 0:512], AF.Ln)
                    sc.activation(rows(R_L1), rows(R_L1), AF.Exp, scale=-1.0)
                    return sc.activation(rows(R_L2), rows(R_L2), AF.Exp,
                                         scale=-1.0)
                t_rc = pl.op("scalar", frc, waits=[(mm, t_pv[nk - 1])], inc=ex)
                den_gate = [(ex, t_rc)]

                def fbc2(t):
                    t.matmul(PSW[0][:, 0:512], ONESR[:, :], rows(R_L1),
                             start=True, stop=True)
                    return t.matmul(PSW[0][:, 512:1024], ONESR[:, :],
                                    rows(R_L2), start=True, stop=True)
                t_bc = pl.op("tensor", fbc2,
                             waits=[(ex, t_rc)] + bank[0] + bank[1], inc=mm)
                t_bcc = pl.op("scalar", lambda sc: sc.copy(
                    BCS[:, :], PSW[0][:, :]),
                    waits=[(mm, t_bc)] + bcs_gate, inc=ex)
                bank[0] = bank[1] = [(ex, t_bcc)]

                def fcm(v, h0=h0, h1=h1):
                    v.tensor_mul(CTX[:, 512 * h0:512 * (h0 + 1)],
                                 PSW[2][:, 0:512], BCS[:, 0:512])
                    return v.tensor_mul(CTX[:, 512 * h1:512 * (h1 + 1)],
                                        PSW[2][:, 512:1024],
                                        BCS[:, 512:1024])
                t_cm = pl.op("vector", fcm,
                             waits=[(ex, t_bcc), (mm, t_pv[nk - 1])] +
                             ctx_sb_gate, inc=ve)
                ctxb_gate[0] = ctxb_gate[1] = [(ve, t_cm)]
                bcs_gate = [(ve, t_cm)]
                last_cm = t_cm
                yield

            oproj_last = 0
            if wob_key is not None:
                while wob_key not in CVT:
                    yield ('need', ("cvt", wob_key))
                sh["d2d"] = CVT[wob_key][1]
            for st in range(4):
                wp = wseq[0] % len(ws_bufs)
                wseq[0] += 1
                wbuf, wsem = ws_bufs[wp]
                t_w = pl.op("sync", lambda sy, wbuf=wbuf, st=st: sy.dma_start(
                    wbuf[:, :], wob[128 * st:128 * (st + 1), :]),
                    waits=ws_gate[wp] + [(S_D2D, sh["d2d"])],
                    inc=wsem, inck=16)
                for oc in range(4):
                    si = (st * 4 + oc) % 4
                    sl = PSW[si // 2][:, 512 * (si % 2):512 * (si % 2 + 1)]

                    def fo(t, wbuf=wbuf, oc=oc, sl=sl):
                        for c in range(CI):
                            ins = t.matmul(
                                sl, wbuf[:, 512 * c + 128 * oc:
                                         512 * c + 128 * (oc + 1)],
                                CTX[:, 512 * c:512 * (c + 1)],
                                start=(c == 0), stop=(c == CI - 1))
                        return ins
                    t_mm = pl.op("tensor", fo,
                                 waits=[(wsem, t_w), (ve, last_cm)] +
                                 bank[si], inc=mm)
                    oproj_last = t_mm
                    ws_gate[wp] = [(mm, t_mm)]
                    ei = (st * 4 + oc) % 2
                    orow = 128 * (4 * st + oc)
                    ocol = TT * tk_i
                    if out_s32:
                        t_ev = pl.op("vector", lambda v, sl=sl, ei=ei:
                                     v.tensor_copy(
                            EVF[:, 512 * ei:512 * (ei + 1)], sl),
                            waits=[(mm, t_mm)] + evf_gate[ei], inc=ve)
                        dst = T["s32"]
                    else:
                        xi = (st * 4 + oc) % 4
                        t_x = pl.op("sync", lambda sy, xi=xi, orow=orow,
                                    ocol=ocol: sy.dma_start(
                            XR[:, 512 * xi:512 * (xi + 1)],
                            T["xn32own"][orow:orow + 128, ocol:ocol + 512]),
                            waits=xr_gate[xi], inc=S_XR[xi], inck=16)
                        t_ev = pl.op("vector", lambda v, sl=sl, ei=ei, xi=xi:
                                     v.tensor_add(
                            EVF[:, 512 * ei:512 * (ei + 1)], sl,
                            XR[:, 512 * xi:512 * (xi + 1)]),
                            waits=[(mm, t_mm), (S_XR[xi], t_x)] +
                            evf_gate[ei], inc=ve)
                        xr_gate[xi] = [(ve, t_ev)]
                        dst = T["u32"]
                    bank[si] = [(ve, t_ev)]
                    t_st = pl.op("scalar", lambda sc, ei=ei, orow=orow,
                                 ocol=ocol, dst=dst: sc.dma_start(
                        dst[orow:orow + 128, ocol:ocol + 512],
                        EVF[:, 512 * ei:512 * (ei + 1)]),
                        waits=[(ve, t_ev)], inc=S_EV[ei], inck=16)
                    evf_gate[ei] = [(S_EV[ei], t_st)]
                    yield
            ctx_sb_gate = [(mm, oproj_last)]

    def groups3_mk():
        groups3 = []
        for s in range(2):
            groups3.append(dict(
                act=T["y16all"], a0=2048 * s, an=2048,
                jobs=[dict(kind="feat", w=T["wk2b"], nstrip=4, out=T["kt2"],
                           wbf16=True, wkey="wk2",
                           ocol0=2048 * s),
                      dict(kind="tok", w=T["wv2b"], nstrip=4, out=T["v2"],
                           wbf16=True, wkey="wv2")]))
        groups3.append(dict(
            act=T["y16own"], a0=0, an=TOWN,
            jobs=[dict(kind="feat", w=T["wq2"], nstrip=4, out=T["qt2"],
                       ocol0=0)]))
        return groups3

    def phase_b(pl, evt):
        sh = dict(d2d=0)
        return [(gen_d2d_b(pl, evt), 1),
                (gen_attn(pl, evt, sh, T["qt1"], T["kt1"], T["v1"],
                          T["wo1b"], True, wob_key="wo1"), 8)]

    run_merged(phase_b)

    def phase_b3(pl, evt):
        return [(gen_proj(pl, evt, groups3_mk(), a1_elems=16384,
                          a1_n=2), 1)]

    run_merged(phase_b3)

    def phase_b4(pl, evt):
        sh = dict(d2d=0)
        return [(gen_d2d_b4(pl, evt), 1),
                (gen_attn(pl, evt, sh, T["qt2"], T["kt2"], T["v2"],
                          T["wo2b"], False, wob_key="wo2"), 8)]

    run_merged(phase_b4)


    def phase_c(pl, evt):
        tiles = [dict(src=T["u32"], i=i, out16=T["z16"], name="z",
                      sb_cols=(V_L2S, V_L2B)) for i in range(4)]
        groups6 = []
        for s in range(2):
            groups6.append(dict(
                act=T["z16"], a0=1024 * s, an=1024,
                gates=[("st16", "z", 2 * s), ("st16", "z", 2 * s + 1)],
                jobs=[dict(kind="feat", w=T["wf1b"], nstrip=16, out=T["h16"],
                           gelu=True, wbf16=True, wkey="wf1",
                           ocol0=1024 * s)]))
        return [(gen_d2d_c(pl, evt), 1), (gen_ln(pl, evt, tiles), 1),
                (gen_proj(pl, evt, groups6, a1_elems=16384, a1_n=1), 6)]

    run_merged(phase_c)
    D2D[0] = S_D2D.n


    def final_phase():
        pl = Plan()
        esl = ExitStack()
        HB = esl.enter_context(nc.sbuf_tensor(uid("hb"), [128, 32768], BF16))
        MB = esl.enter_context(nc.sbuf_tensor(uid("mb"), [128, 2 * 512],
                                               mybir.dt.uint8))
        FA = esl.enter_context(nc.sbuf_tensor(uid("fa"), [128, 16 * 512],
                                               F32))
        FO = esl.enter_context(nc.sbuf_tensor(uid("fo"), [128, 2 * 512], F32))
        ws_gate = [[], []]
        fo_gate = [[], []]
        hb_gate = []
        mb_gate = []
        fa_gate = [[], []]
        slot_gate = [[], []]
        lr_gate = []
        rows_gate = []

        for tt_i in range(4):
            tc = slice(TT * tt_i, TT * (tt_i + 1))
            t_h = pl.op("sync", lambda sy, tc=tc: sy.dma_start(
                HB[:, :].rearrange("p (c t) -> p c t", c=64),
                T["h16"][:, tc].rearrange("(c p) t -> p c t", p=128)),
                waits=hb_gate, inc=S_A1, inck=16)
            pl.op("sync", lambda sy, tt_i=tt_i: sy.dma_start(
                rows(R_L1), T["mrows"][0:1, TT * tt_i:TT * (tt_i + 1)]),
                waits=rows_gate, inc=S_ROW, inck=16)
            t_mr = pl.op("sync", lambda sy, tt_i=tt_i: sy.dma_start(
                rows(R_L2),
                T["mrows"][0:1, TOWN + TT * tt_i:TOWN + TT * (tt_i + 1)]),
                inc=S_ROW, inck=16)

            def fmb(t):
                t.matmul(PSW[3][:, 0:512], ONESR[:, :], rows(R_L1),
                         start=True, stop=True)
                return t.matmul(PSW[3][:, 512:1024], ONESR[:, :], rows(R_L2),
                                start=True, stop=True)
            t_mb = pl.op("tensor", fmb, waits=[(S_ROW, t_mr)] + mb_gate,
                         inc=mm)
            t_mbc = pl.op("scalar", lambda sc: (
                sc.copy(MB[:, 0:512], PSW[3][:, 0:512]),
                sc.copy(MB[:, 512:1024], PSW[3][:, 512:1024]))[-1],
                waits=[(mm, t_mb)] + mb_gate, inc=ex)
            rows_gate = [(mm, t_mb)]
            mb_next = []
            for oc in range(CI):
                wp = (tt_i * CI + oc) % 2
                t_w = pl.op("sync", lambda sy, wp=wp, oc=oc: sy.dma_start(
                    WS[wp][:, :], T["wf2b"][128 * oc:128 * (oc + 1), :]),
                    waits=ws_gate[wp] + [(S_D2D, D2D[0])], inc=S_WS[wp],
                    inck=16)
                si = oc % 2
                sl = PSW[si][:, 0:512]

                def fm(t, wp=wp, sl=sl):
                    for c in range(64):
                        ins = t.matmul(sl, WS[wp][:, 128 * c:128 * (c + 1)],
                                       HB[:, 512 * c:512 * (c + 1)],
                                       start=(c == 0), stop=(c == 63))
                    return ins
                t_mm2 = pl.op("tensor", fm,
                              waits=[(S_WS[wp], t_w), (S_A1, t_h)] +
                              slot_gate[si], inc=mm)
                ws_gate[wp] = [(mm, t_mm2)]
                slr = PSW[2][:, 0:512]

                def flr(t, oc=oc, tc=tc):
                    return t.matmul(slr, WUP[:, 128 * oc:128 * (oc + 1)],
                                    T1[:, tc], start=True, stop=True)
                t_lr = pl.op("tensor", flr, waits=lr_gate, inc=mm)
                orow = slice(128 * oc, 128 * (oc + 1))
                fop = (tt_i * CI + oc) % 2
                B0 = 4096 * fop

                def fld(sy, orow=orow, tc=tc, B0=B0):
                    sy.dma_start(FA[:, B0:B0 + 512],
                                 T["xn32own"][orow, tc]).then_inc(S_FL.h, 16)
                    sy.dma_start(FA[:, B0 + 512:B0 + 1024],
                                 T["u32"][orow, tc]).then_inc(S_FL.h, 16)
                    sy.dma_start(FA[:, B0 + 1024:B0 + 1536],
                                 T["xto"][orow, tc]).then_inc(S_FL.h, 16)
                    return sy.dma_start(FA[:, B0 + 1536:B0 + 2048],
                                        T["s32"][orow, tc])
                S_FL.n += 48
                t_fl = pl.op("sync", fld, waits=fa_gate[fop], inc=S_FL,
                             inck=16)

                def fev(v, sl=sl, oc=oc, B0=B0):
                    v.tensor_add(FA[:, B0 + 2048:B0 + 2560], sl,
                                 FA[:, B0 + 512:B0 + 1024])
                    v.tensor_add(FA[:, B0 + 2560:B0 + 3072], slr,
                                 FA[:, B0:B0 + 512])
                    v.tensor_scalar(FA[:, B0 + 3072:B0 + 3584],
                                    FA[:, B0:B0 + 512],
                                    VECS[:, V_SB * 16 + oc:V_SB * 16 + oc + 1],
                                    None, ALU.add)
                    v.select(FA[:, B0 + 3584:B0 + 4096], MB[:, 512:1024],
                             FA[:, B0 + 2560:B0 + 3072],
                             FA[:, B0 + 2048:B0 + 2560])
                    return v.select(FA[:, B0 + 2048:B0 + 2560], MB[:, 0:512],
                                    FA[:, B0 + 3072:B0 + 3584],
                                    FA[:, B0 + 3584:B0 + 4096])
                t_ev = pl.op("vector", fev,
                             waits=[(mm, t_mm2), (mm, t_lr), (S_FL, t_fl),
                                    (ex, t_mbc)], inc=ve)
                slot_gate[si] = [(ve, t_ev)]
                lr_gate = [(ve, t_ev)]
                mb_next = [(ve, t_ev)]
                t_g = pl.op("scalar", lambda sc, B0=B0: sc.activation(
                    FA[:, B0 + 3584:B0 + 4096],
                    FA[:, B0 + 2048:B0 + 2560], AF.Silu),
                    waits=[(ve, t_ev)], inc=ex)

                def ffin(v, fop=fop, B0=B0):
                    v.tensor_mul(FA[:, B0 + 2048:B0 + 2560],
                                 FA[:, B0 + 3584:B0 + 4096],
                                 FA[:, B0 + 1536:B0 + 2048])
                    return v.tensor_add(FO[:, 512 * fop:512 * (fop + 1)],
                                        FA[:, B0 + 2048:B0 + 2560],
                                        FA[:, B0 + 1024:B0 + 1536])
                t_f = pl.op("vector", ffin,
                            waits=[(ex, t_g)] + fo_gate[fop], inc=ve)
                fa_gate[fop] = [(ve, t_f)]
                t_st = pl.op("scalar", lambda sc, fop=fop, orow=orow, tc=tc:
                             sc.dma_start(T["outT"][orow, tc],
                                          FO[:, 512 * fop:512 * (fop + 1)]),
                             waits=[(ve, t_f)], inc=S_EV[fop], inck=16)
                fo_gate[fop] = [(S_EV[fop], t_st)]
            hb_gate = [(mm, mm.n)]
            mb_gate = mb_next
        pl.op("sync", lambda sy: sy.nop(), waits=drain_waits())
        pl.run(nc)
        esl.close()

    final_phase()
    return nc, es


def _block_w(wt, nstrip, wo):
    din, dout = wt.shape
    ci = din // 128
    a = wt.reshape(ci, 128, nstrip, wo).transpose(2, 1, 0, 3)
    return np.ascontiguousarray(a.reshape(nstrip * 128, ci * wo))


def _pack_vec(v):
    return np.ascontiguousarray(v.reshape(16, 128).T)


def kernel(L0_attn_hidden_states, L0_attn_w_qkv, L0_attn_w_o,
           L0_ffn_simple_bias, L0_ffn_w_down, L0_ffn_w_up, L0_ffn_w_router,
           L0_ffn_ln1_scale, L0_ffn_ln1_bias, L0_ffn_wq, L0_ffn_wk,
           L0_ffn_wv, L0_ffn_wo, L0_ffn_ln2_scale, L0_ffn_ln2_bias,
           L0_ffn_w_ff1, L0_ffn_w_ff2):
    f32 = lambda a: np.asarray(a, dtype=np.float32)
    x = f32(L0_attn_hidden_states)
    qkv = f32(L0_attn_w_qkv)

    wq1 = _block_w(np.ascontiguousarray(qkv[0:D].T), 4, 512)
    wk1 = _block_w(np.ascontiguousarray(qkv[D:2 * D].T), 4, 512)
    wv1 = _block_w(np.ascontiguousarray(qkv[2 * D:3 * D].T), 4, 512)
    wo1 = _block_w(np.ascontiguousarray(f32(L0_attn_w_o).T), 4, 512)
    wq2 = _block_w(np.ascontiguousarray(f32(L0_ffn_wq).T), 4, 512)
    wk2 = _block_w(np.ascontiguousarray(f32(L0_ffn_wk).T), 4, 512)
    wv2 = _block_w(np.ascontiguousarray(f32(L0_ffn_wv).T), 4, 512)
    wo2 = _block_w(np.ascontiguousarray(f32(L0_ffn_wo).T), 4, 512)
    wf1 = _block_w(np.ascontiguousarray(f32(L0_ffn_w_ff1).T), 16, 512)
    wf2 = _block_w(np.ascontiguousarray(f32(L0_ffn_w_ff2).T), 16, 128)

    vecs = np.stack([_pack_vec(f32(v)) for v in
                     [L0_ffn_simple_bias, L0_ffn_ln1_scale, L0_ffn_ln1_bias,
                      L0_ffn_ln2_scale, L0_ffn_ln2_bias]], axis=1)
    vecs = np.ascontiguousarray(vecs.reshape(128, 80))
    wrt = np.ascontiguousarray(
        f32(L0_ffn_w_router).T.reshape(16, 128, 3).transpose(1, 0, 2)
        .reshape(128, 48))
    wdt = np.ascontiguousarray(
        f32(L0_ffn_w_down).T.reshape(16, 128, RK).transpose(1, 0, 2)
        .reshape(128, 16 * RK))
    wupt = np.ascontiguousarray(f32(L0_ffn_w_up).T)

    cmasks = []
    for par in range(2):
        p = np.arange(128)[:, None]
        f = np.arange(512)[None, :]
        m = np.empty((128, 8 * 512), dtype=np.float32)
        for d in range(8):
            allowed = (128 * d + p) <= (128 * (2 * (f // 128) + par) + f % 128)
            m[:, 512 * d:512 * (d + 1)] = np.where(allowed, 0.0, NEG)
        cmasks.append(m.astype(ml_dtypes.bfloat16))

    nc, es = build_program()
    shared = dict(vecs=vecs, wrt=wrt, wdt=wdt, wupt=wupt,
                  wq1=wq1, wk1=wk1, wv1=wv1, wo1=wo1, wq2=wq2, wk2=wk2,
                  wv2=wv2, wo2=wo2, wf1=wf1, wf2=wf2)
    in_maps = []
    for c in range(8):
        b, par = c // 2, c % 2
        gsl = np.concatenate(
            [np.arange(128 * (2 * L + par), 128 * (2 * L + par) + 128)
             for L in range(16)])
        m = dict(shared)
        m["xt"] = np.ascontiguousarray(x[b].T)
        m["xto"] = np.ascontiguousarray(x[b][gsl].T)
        m["cmask"] = cmasks[par]
        in_maps.append(m)

    res = run_bass_kernel_spmd(nc, in_maps, core_ids=list(range(8)))
    es.close()

    out = np.empty((B, S, D), dtype=np.float32)
    for c in range(8):
        b, par = c // 2, c % 2
        ot = res.results[c]["outT"]
        for L in range(16):
            g = 2 * L + par
            out[b, 128 * g:128 * (g + 1), :] = \
                ot[:, 128 * L:128 * (L + 1)].T
    return out


if __name__ == "__main__":
    ncx, esx = build_program()
    print("built ok")
    esx.close()


# version 26
# speedup vs baseline: 1.0282x; 1.0039x over previous
import numpy as np
import ml_dtypes
from contextlib import ExitStack

import concourse.bass as bass
import concourse.mybir as mybir
from concourse.bass_utils import run_bass_kernel_spmd

F32 = mybir.dt.float32
BF16 = mybir.dt.bfloat16
AF = mybir.ActivationFunctionType
ALU = mybir.AluOpType

B, S, D, HN, DH, FF, RK = 4, 4096, 2048, 16, 128, 8192, 32
CI = D // 128
TT = 512
TOWN = 2048
SCALE = float(1.0 / np.sqrt(128.0))
INV_D = float(1.0 / D)
EPS = 1e-5
NEG = -1.0e30

V_SB, V_L1S, V_L1B, V_L2S, V_L2B = 0, 1, 2, 3, 4

ENGS = ["sync", "gpsimd", "tensor", "vector", "scalar"]


class Sem:
    def __init__(self, nc, es, name):
        self.h = es.enter_context(nc.semaphore(name))
        self.name = name
        self.n = 0


class Plan:

    def __init__(self):
        self.q = {e: [] for e in ENGS}
        self.memo = {}

    def op(self, ek, build, waits=(), inc=None, inck=1):
        fw = []
        for s, t in waits:
            if not t or t <= 0:
                continue
            key = (ek, s.name)
            if self.memo.get(key, 0) >= t:
                continue
            self.memo[key] = t
            fw.append((s.h, t))
        tag = None
        if inc is not None:
            inc.n += inck
            tag = inc.n

        def fn(eng, fw=fw, build=build, inc=inc, inck=inck):
            for h, t in fw:
                eng.wait_ge(h, t)
            ins = build(eng)
            if inc is not None:
                ins.then_inc(inc.h, inck)

        self.q[ek].append(fn)
        return tag

    def run(self, nc):
        with nc.Block() as blk:
            for e in ENGS:
                fns = self.q[e]
                if fns:
                    getattr(blk, e)(lambda eng, fns=fns: [f(eng) for f in fns])


def drive(pl, gens, evt):
    norm = [(iter(g[0]), g[1]) if isinstance(g, tuple) else (iter(g), 1)
            for g in gens]
    active = [list(x) for x in norm]
    blocked = {}
    progress = True
    while active:
        if not progress:
            raise RuntimeError(
                f"drive deadlock: blocked on {set(blocked.values())}")
        progress = False
        for ent in list(active):
            g, w = ent
            if id(g) in blocked and blocked[id(g)] not in evt:
                continue
            blocked.pop(id(g), None)
            for _ in range(w):
                try:
                    r = next(g)
                    progress = True
                    if isinstance(r, tuple) and r and r[0] == 'need':
                        if r[1] not in evt:
                            blocked[id(g)] = r[1]
                            break
                except StopIteration:
                    active.remove(ent)
                    progress = True
                    break


def build_program(taps=()):
    nc = bass.Bass("TRN2", target_bir_lowering=False)
    es = ExitStack()
    T = {}
    uidc = [0]

    def uid(n):
        uidc[0] += 1
        return f"{n}_{uidc[0]}"

    def dram(name, shape, dtype, kind="Internal"):
        kd = "ExternalOutput" if name in taps else kind
        T[name] = nc.dram_tensor(name, shape, dtype, kind=kd).ap()
        return T[name]

    dram("xt", [D, S], F32, "ExternalInput")
    dram("xto", [D, TOWN], F32, "ExternalInput")
    dram("vecs", [128, 80], F32, "ExternalInput")
    dram("wrt", [128, 48], F32, "ExternalInput")
    dram("wdt", [128, 512], F32, "ExternalInput")
    dram("wupt", [RK, D], F32, "ExternalInput")
    dram("cmask", [128, 8 * 512], BF16, "ExternalInput")
    for nm in ["wq1", "wk1", "wv1", "wo1", "wq2", "wk2", "wv2", "wo2"]:
        dram(nm, [4 * 128, CI * 512], F32, "ExternalInput")
    dram("wf1", [16 * 128, CI * 512], F32, "ExternalInput")
    dram("wf2", [16 * 128, 64 * 128], F32, "ExternalInput")

    dram("xn16all", [D, S], BF16)
    dram("y16all", [D, S], BF16)
    dram("xn32own", [D, TOWN], F32)
    dram("xn16own", [D, TOWN], BF16)
    dram("y16own", [D, TOWN], BF16)
    dram("kt1", [D, S], BF16)
    dram("v1", [S, D], BF16)
    dram("qt1", [D, TOWN], BF16)
    dram("kt2", [D, S], BF16)
    dram("v2", [S, D], BF16)
    dram("qt2", [D, TOWN], BF16)
    dram("s32", [D, TOWN], F32)
    dram("u32", [D, TOWN], F32)
    dram("z16", [D, TOWN], BF16)
    dram("h16", [FF, TOWN], BF16)
    dram("wk1b", [4 * 128, CI * 512], BF16)
    dram("wv1b", [4 * 128, CI * 512], BF16)
    dram("wk2b", [4 * 128, CI * 512], BF16)
    dram("wv2b", [4 * 128, CI * 512], BF16)
    dram("wo1b", [4 * 128, CI * 512], BF16)
    dram("wo2b", [4 * 128, CI * 512], BF16)
    dram("wf1b", [16 * 128, CI * 512], BF16)
    dram("wf2b", [16 * 128, 64 * 128], BF16)
    dram("mrows", [1, 2 * TOWN], F32)
    dram("outT", [D, TOWN], F32, "ExternalOutput")

    semnames = (["dx", "dw", "ds", "mm", "ex", "ve", "ini", "d2d", "a1",
                 "a1b", "row", "st16", "st32", "sty", "fl"] +
                [f"cv{i}" for i in range(5)] +
                [f"xb{i}" for i in range(2)] + [f"ws{i}" for i in range(4)] +
                [f"ev{i}" for i in range(4)] + [f"ld{i}" for i in range(2)] +
                [f"xr{i}" for i in range(4)])
    SM = {s: Sem(nc, es, s) for s in semnames}
    dx, dw, ds, mm, ex, ve = (SM[s] for s in ["dx", "dw", "ds", "mm", "ex", "ve"])
    (S_INI, S_D2D, S_A1, S_A1B, S_ROW, S_ST16, S_ST32, S_STY,
     S_FL) = (SM[s] for s in ["ini", "d2d", "a1", "a1b", "row", "st16",
                              "st32", "sty", "fl"])
    S_CV = [SM[f"cv{i}"] for i in range(5)]
    S_XB = [SM[f"xb{i}"] for i in range(2)]
    S_WS = [SM[f"ws{i}"] for i in range(4)]
    S_EV = [SM[f"ev{i}"] for i in range(4)]
    S_LD = [SM[f"ld{i}"] for i in range(2)]
    S_XR = [SM[f"xr{i}"] for i in range(4)]
    ALLDMA = ([S_INI, S_D2D, S_A1, S_A1B, S_ROW, S_ST16, S_ST32, S_STY,
               S_FL] +
              S_CV + S_XB + S_WS + S_EV + S_LD + S_XR)

    def drain_waits():
        return [(s, s.n) for s in ALLDMA if s.n > 0]


    sb = lambda name, shape, dt: es.enter_context(nc.sbuf_tensor(name, shape, dt))
    WS = [sb("ws0", [128, 8192], BF16), sb("ws1", [128, 8192], BF16)]
    ONESC = sb("onesc", [128, 1], BF16)
    ONESR = sb("onesr", [1, 128], F32)
    EPSC = sb("epsc", [128, 1], F32)
    VECS = sb("vecsb", [128, 80], F32)
    WRT = sb("wrtb", [128, 48], F32)
    WDT = sb("wdtb", [128, 512], BF16)
    WUP = sb("wupb", [RK, D], BF16)
    CMK = sb("cmkb", [128, 8 * 512], BF16)
    IDN = sb("idn", [128, 128], BF16)
    T1 = sb("t1res", [RK, TOWN], BF16)
    ROWS = sb("rows", [1, 3584], F32)
    L3 = sb("l3", [3, 512], F32)
    PSW = [es.enter_context(nc.psum_tensor(f"psw{i}", [128, 1024], F32))
           for i in range(4)]

    R_MEAN, R_T2, R_MSQ, R_VAR, R_MR, R_L1, R_L2 = (
        0, 512, 1024, 1536, 2048, 2560, 3072)
    R_TMP = R_T2

    def rows(off, n=512):
        return ROWS[:, off:off + n]

    def bcast_ap(psw, half):
        return bass.AP(psw, 512 * half, [[1024, 128], [0, CI], [1, 512]])

    def wide_ap(psw):
        return bass.AP(psw, 0, [[1024, 128], [1, 1024]])

    pl = Plan()
    pl.op("gpsimd", lambda g: g.memset(ONESC[:, :], 1.0), inc=ex)
    pl.op("gpsimd", lambda g: g.memset(ONESR[:, :], 1.0), inc=ex)
    pl.op("gpsimd", lambda g: g.memset(EPSC[:, :], EPS), inc=ex)
    pl.op("gpsimd", lambda g: g.memset(IDN[:, :], 1.0), inc=ex)
    t_idn = pl.op("gpsimd", lambda g: g.affine_select(
        IDN[:, :], IDN[:, :], [[1, 128]], ALU.is_equal, 0.0,
        base=0, channel_multiplier=-1), inc=ex)
    pl.op("gpsimd", lambda g: g.dma_start(WDT[:, :], T["wdt"]),
          inc=S_INI, inck=16)
    pl.op("gpsimd", lambda g: g.dma_start(WUP[:, :], T["wupt"]),
          inc=S_INI, inck=16)
    pl.op("sync", lambda sy: sy.dma_start(VECS[:, :], T["vecs"]),
          inc=S_INI, inck=16)
    pl.op("sync", lambda sy: sy.dma_start(WRT[:, :], T["wrt"]),
          inc=S_INI, inck=16)
    pl.op("sync", lambda sy: sy.dma_start(CMK[:, :], T["cmask"]),
          inc=S_INI, inck=16)
    pl.op("sync", lambda sy: sy.nop(), waits=[(S_INI, S_INI.n), (ex, t_idn)])
    pl.run(nc)

    def gen_ln(pl, evt, tiles):
        esl = ExitStack()
        XB = esl.enter_context(nc.sbuf_tensor(uid("xb"), [128, 8192], F32))
        XN = esl.enter_context(nc.sbuf_tensor(uid("xnb"), [128, 8192], F32))
        B16 = esl.enter_context(nc.sbuf_tensor(uid("b16"), [128, 16384], BF16))
        evt.setdefault("_esl", []).append(esl)
        x16 = B16[:, 0:8192]
        sq16 = B16[:, 8192:16384]
        x16v = x16.rearrange("p (c t) -> p c t", c=CI)
        sq16v = sq16.rearrange("p (c t) -> p c t", c=CI)
        xnv = XN[:, :].rearrange("p (c t) -> p c t", c=CI)

        xb_gate = []
        xn_gate = []
        x16_gate = []
        sq_gate = []
        rowb_gate = []
        bc_gate = []
        msk_prev = 0

        for job in tiles:
            src, i = job["src"], job["i"]
            out16, out32 = job["out16"], job.get("out32")
            sb_cols = job.get("sb_cols")
            second = job.get("second")
            router = job.get("router", False)
            name = job["name"]
            csl = slice(TT * i, TT * (i + 1))
            t_ld = pl.op("sync", lambda sy, src=src, csl=csl: sy.dma_start(
                XB[:, :].rearrange("p (c t) -> p c t", c=CI),
                src[:, csl].rearrange("(c p) t -> p c t", p=128)),
                waits=xb_gate, inc=S_XB[0], inck=16)
            t_x16 = pl.op("scalar", lambda sc: sc.activation(
                x16, XB[:, :], AF.Copy),
                waits=[(S_XB[0], t_ld)] + x16_gate, inc=ex)
            t_sq = pl.op("scalar", lambda sc: sc.activation(
                sq16, XB[:, :], AF.Square), waits=sq_gate, inc=ex)
            yield

            def fsum(t):
                for c in range(CI):
                    t.matmul(PSW[2][0:1, 0:512], ONESC[:, :], x16v[:, c, :],
                             start=(c == 0), stop=(c == CI - 1))
                for c in range(CI):
                    ins = t.matmul(PSW[2][0:1, 512:1024], ONESC[:, :],
                                   sq16v[:, c, :], start=(c == 0),
                                   stop=(c == CI - 1))
                return ins
            t_sum = pl.op("tensor", fsum,
                          waits=[(ex, t_x16), (ex, t_sq)] + rowb_gate, inc=mm)

            def frows(v):
                v.tensor_scalar(rows(R_MEAN), PSW[2][0:1, 0:512], INV_D, None,
                                ALU.mult)
                v.tensor_scalar(rows(R_T2), PSW[2][0:1, 512:1024], INV_D, None,
                                ALU.mult)
                v.tensor_mul(rows(R_MSQ), rows(R_MEAN), rows(R_MEAN))
                return v.tensor_sub(rows(R_VAR), rows(R_T2), rows(R_MSQ))
            t_rows = pl.op("vector", frows, waits=[(mm, t_sum)], inc=ve)
            rowb_gate = [(ve, t_rows)]

            def frstd(sc):
                sc.activation(rows(R_T2), rows(R_VAR), AF.Ln,
                              bias=EPSC[0:1, 0:1])
                return sc.activation(rows(R_MSQ), rows(R_T2), AF.Exp,
                                     scale=-0.5)
            t_rstd = pl.op("scalar", frstd, waits=[(ve, t_rows)], inc=ex)

            def fmr(v, second=second):
                ins = v.tensor_mul(rows(R_MR), rows(R_MEAN), rows(R_MSQ))
                if second is not None:
                    v.tensor_mul(rows(R_T2), rows(R_MSQ), rows(R_MSQ))
                    ins = v.tensor_mul(rows(R_T2), rows(R_VAR), rows(R_T2))
                return ins
            t_mr = pl.op("vector", fmr, waits=[(ex, t_rstd)], inc=ve)

            def fbc(t):
                t.matmul(PSW[3][:, 0:512], ONESR[:, :], rows(R_MSQ),
                         start=True, stop=True)
                return t.matmul(PSW[3][:, 512:1024], ONESR[:, :], rows(R_MR),
                                start=True, stop=True)
            t_bc = pl.op("tensor", fbc, waits=[(ve, t_mr)] + bc_gate, inc=mm)

            def fxn(v):
                xbv = XB[:, :].rearrange("p (c t) -> p c t", c=CI)
                v.tensor_mul(xnv, xbv, bcast_ap(PSW[3], 0))
                return v.tensor_sub(xnv, xnv, bcast_ap(PSW[3], 1))
            t_xn = pl.op("vector", fxn, waits=[(mm, t_bc)] + xn_gate, inc=ve)
            bc_gate = [(ve, t_xn)]
            yield

            if sb_cols is None:
                t_o16 = pl.op("scalar", lambda sc: sc.activation(
                    x16, XN[:, :], AF.Copy),
                    waits=[(ve, t_xn), (mm, t_sum)], inc=ex)
                o16_sem = ex
            else:
                scol, bcol = sb_cols
                def fo16(v, scol=scol, bcol=bcol):
                    for c in range(CI):
                        ins = v.tensor_scalar(
                            x16[:, TT * c:TT * (c + 1)],
                            XN[:, TT * c:TT * (c + 1)],
                            VECS[:, scol * 16 + c:scol * 16 + c + 1],
                            VECS[:, bcol * 16 + c:bcol * 16 + c + 1],
                            ALU.mult, ALU.add)
                    return ins
                t_o16 = pl.op("vector", fo16,
                              waits=[(ve, t_xn), (mm, t_sum)], inc=ve)
                o16_sem = ve
            t_st16 = pl.op("scalar", lambda sc, out16=out16, csl=csl:
                           sc.dma_start(
                out16[:, csl].rearrange("(c p) t -> p c t", p=128), x16v),
                waits=[(o16_sem, t_o16)], inc=S_ST16, inck=16)
            x16_gate = [(S_ST16, t_st16)]
            xn_gate = [(o16_sem, t_o16)]
            evt[("st16", name, i)] = (S_ST16, t_st16)
            if out32 is not None:
                t_st32 = pl.op("scalar", lambda sc, out32=out32, csl=csl:
                               sc.dma_start(
                    out32[:, csl].rearrange("(c p) t -> p c t", p=128), xnv),
                    waits=[(ve, t_xn)], inc=S_ST32, inck=16)
                xn_gate.append((S_ST32, t_st32))
                evt[("st32", name, i)] = (S_ST32, t_st32)
            yield

            if router:
                def frt(t):
                    for c in range(CI):
                        ins = t.matmul(PSW[2][0:3, 0:512],
                                       WRT[:, 3 * c:3 * (c + 1)],
                                       XN[:, TT * c:TT * (c + 1)],
                                       start=(c == 0), stop=(c == CI - 1))
                    return ins
                t_rt = pl.op("tensor", frt,
                             waits=[(ve, t_xn), (ve, t_rows)], inc=mm)
                xn_gate.append((mm, t_rt))
                t_rl = pl.op("scalar", lambda sc: sc.copy(
                    L3[:, :], PSW[2][0:3, 0:512]), waits=[(mm, t_rt)], inc=ex)
                pl.op("sync", lambda sy: sy.dma_start(rows(R_L1), L3[1:2, :]),
                      waits=[(ex, t_rl), (ve, msk_prev)], inc=S_ROW, inck=16)
                t_rmv = pl.op("sync", lambda sy: sy.dma_start(
                    rows(R_L2), L3[2:3, :]), inc=S_ROW, inck=16)

                def fmask(v):
                    v.tensor_tensor(rows(R_MEAN), L3[0:1, :], rows(R_L1),
                                    ALU.is_ge)
                    v.tensor_tensor(rows(R_VAR), L3[0:1, :], rows(R_L2),
                                    ALU.is_ge)
                    v.tensor_mul(rows(R_MEAN), rows(R_MEAN), rows(R_VAR))
                    v.tensor_tensor(rows(R_VAR), rows(R_L1), rows(R_L2),
                                    ALU.is_ge)
                    v.tensor_scalar(rows(R_MSQ), rows(R_MEAN), -1.0, 1.0,
                                    ALU.mult, ALU.add)
                    return v.tensor_mul(rows(R_VAR), rows(R_MSQ), rows(R_VAR))
                t_msk = pl.op("vector", fmask,
                              waits=[(S_ROW, t_rmv), (ex, t_rl)], inc=ve)
                t_ms = pl.op("scalar", lambda sc, i=i: sc.dma_start(
                    T["mrows"][0:1, TT * i:TT * (i + 1)], rows(R_MEAN)),
                    waits=[(ve, t_msk)], inc=S_ROW, inck=16)
                t_ms = pl.op("scalar", lambda sc, i=i: sc.dma_start(
                    T["mrows"][0:1, TOWN + TT * i:TOWN + TT * (i + 1)],
                    rows(R_VAR)), inc=S_ROW, inck=16)
                evt[("mrow", i)] = (S_ROW, t_ms)
                msk_prev = t_msk
                rowb_gate = [(ve, t_msk), (ex, t_rl)]
                yield

            if second is not None:
                def frstd2(sc):
                    sc.activation(rows(R_T2), rows(R_T2), AF.Ln,
                                  bias=EPSC[0:1, 0:1])
                    return sc.activation(rows(R_T2), rows(R_T2), AF.Exp,
                                         scale=-0.5)
                t_rstd2 = pl.op("scalar", frstd2, waits=[(ve, t_mr)], inc=ex)
                t_bc2 = pl.op("tensor", lambda t: t.matmul(
                    PSW[3][:, 0:512], ONESR[:, :], rows(R_T2),
                    start=True, stop=True),
                    waits=[(ex, t_rstd2), (ve, t_xn)], inc=mm)
                def fy(v):
                    return v.tensor_mul(
                        XB[:, :].rearrange("p (c t) -> p c t", c=CI),
                        xnv, bcast_ap(PSW[3], 0))
                t_y = pl.op("vector", fy, waits=[(mm, t_bc2)], inc=ve)
                bc_gate = [(ve, t_y)]
                scol, bcol = second["scale_col"], second["bias_col"]

                def fy16(v, scol=scol, bcol=bcol):
                    for c in range(CI):
                        ins = v.tensor_scalar(
                            sq16[:, TT * c:TT * (c + 1)],
                            XB[:, TT * c:TT * (c + 1)],
                            VECS[:, scol * 16 + c:scol * 16 + c + 1],
                            VECS[:, bcol * 16 + c:bcol * 16 + c + 1],
                            ALU.mult, ALU.add)
                    return ins
                t_y16 = pl.op("vector", fy16,
                              waits=[(ve, t_y), (mm, t_sum)], inc=ve)
                t_sty = pl.op("scalar", lambda sc, second=second, csl=csl:
                              sc.dma_start(
                    second["out"][:, csl].rearrange("(c p) t -> p c t", p=128),
                    sq16v), waits=[(ve, t_y16)], inc=S_STY, inck=16)
                evt[("sty", second["name"], i)] = (S_STY, t_sty)
                xb_gate = [(ve, t_y16)]
                sq_gate = [(S_STY, t_sty)]
            else:
                xb_gate = [(ve, t_xn), (ex, t_sq)]
                sq_gate = [(mm, t_sum)]
            yield
        evt[("ln_done", id(tiles))] = True

    def gen_proj(pl, evt, groups, a1_elems=32768, slots=(0, 1, 2, 3),
                 ev_sems=None, a1_sem=None, sh_ws=None, ws_bufs=None):
        if ev_sems is None:
            ev_sems = S_EV
        if a1_sem is None:
            a1_sem = S_A1
        if ws_bufs is None:
            ws_bufs = [(WS[0], S_WS[0]), (WS[1], S_WS[1])]
        sh_ws = dict(ws_gate=[[] for _ in ws_bufs], wseq=[0])
        nslot = len(slots)
        nev = len(ev_sems)
        esl = ExitStack()
        A1 = esl.enter_context(nc.sbuf_tensor(uid("a1"), [128, a1_elems],
                                              BF16))
        EV = esl.enter_context(nc.sbuf_tensor(uid("ev"), [128, nev * 512],
                                              BF16))
        evt.setdefault("_esl", []).append(esl)
        seg_tok = a1_elems // CI
        ws_gate = sh_ws["ws_gate"]
        ev_gate = [[] for _ in range(nev)]
        slot_gate = [[] for _ in range(nslot)]
        a1_gate = []
        a1_last = [0]
        sseq = [0]
        eseq = [0]
        wseq = sh_ws["wseq"]

        def psum_slot(k):
            i = slots[k]
            return PSW[i // 2][:, 512 * (i % 2):512 * (i % 2 + 1)]

        def evac_store(t_mm, make_store, gelu=False):
            ei = eseq[0] % nev
            eseq[0] += 1
            sl_i = (sseq[0] - 1) % nslot
            sl = psum_slot(sl_i)
            if gelu or ei % 2:
                t_ev = pl.op("scalar", lambda sc, sl=sl, ei=ei: sc.activation(
                    EV[:, 512 * ei:512 * (ei + 1)], sl,
                    AF.Gelu_apprx_tanh if gelu else AF.Copy),
                    waits=[(mm, t_mm)] + ev_gate[ei], inc=ex)
                esem = ex
            else:
                t_ev = pl.op("vector", lambda v, sl=sl, ei=ei: v.tensor_copy(
                    EV[:, 512 * ei:512 * (ei + 1)], sl),
                    waits=[(mm, t_mm)] + ev_gate[ei], inc=ve)
                esem = ve
            slot_gate[sl_i] = [(esem, t_ev)]
            t_st = pl.op("scalar", lambda sc, ei=ei: make_store(
                sc, EV[:, 512 * ei:512 * (ei + 1)]),
                waits=[(esem, t_ev)], inc=ev_sems[ei], inck=16)
            ev_gate[ei] = [(ev_sems[ei], t_st)]
            return t_st

        for grp in groups:
            act, a0, an = grp["act"], grp["a0"], grp["an"]
            for key in grp.get("gates", ()):
                while key not in evt:
                    yield ('need', key)
            ggates = [evt[k] for k in grp.get("gates", ())]
            for seg in range((an + seg_tok - 1) // seg_tok):
                sa0 = a0 + seg_tok * seg
                sn = min(seg_tok, a0 + an - sa0)
                nct = sn // TT
                t_a1 = pl.op("sync", lambda sy, act=act, sa0=sa0, sn=sn:
                             sy.dma_start(
                    A1[:, 0:CI * sn].rearrange("p (c t) -> p c t", c=CI),
                    act[:, sa0:sa0 + sn].rearrange("(c p) t -> p c t", p=128)),
                    waits=a1_gate + ggates, inc=a1_sem, inck=16)
                for job in grp["jobs"]:
                    if job.get("wkey"):
                        while job["wkey"] not in CVT:
                            yield ('need', ("cvt", job["wkey"]))
                        job["wgate"] = [CVT[job["wkey"]]]
                    if job["kind"] == "rank":
                        for tt_i in range(nct):
                            sl_i = sseq[0] % nslot
                            sl = psum_slot(sl_i)
                            sseq[0] += 1

                            def fr(t, tt_i=tt_i, sl=sl, sn=sn):
                                for c in range(CI):
                                    ins = t.matmul(
                                        sl[0:RK, :],
                                        WDT[:, RK * c:RK * (c + 1)],
                                        A1[:, c * sn + TT * tt_i:
                                           c * sn + TT * (tt_i + 1)],
                                        start=(c == 0), stop=(c == CI - 1))
                                return ins
                            t_mm = pl.op("tensor", fr,
                                         waits=[(a1_sem, t_a1)] +
                                         slot_gate[sl_i], inc=mm)
                            a1_last[0] = t_mm
                            t_ev = pl.op("vector",
                                         lambda v, sl=sl, tt_i=tt_i, sa0=sa0,
                                         a0=a0: v.tensor_copy(
                                T1[:, sa0 - a0 + TT * tt_i:
                                   sa0 - a0 + TT * (tt_i + 1)],
                                sl[0:RK, :]), waits=[(mm, t_mm)], inc=ve)
                            slot_gate[sl_i] = [(ve, t_ev)]
                            yield
                        continue
                    for st in range(job["nstrip"]):
                        wp = wseq[0] % len(ws_bufs)
                        wseq[0] += 1
                        wbuf, wsem = ws_bufs[wp]
                        wdram = job["w"]
                        if job.get("wbf16"):
                            t_w = pl.op("sync",
                                        lambda sy, wbuf=wbuf, st=st,
                                        wdram=wdram: sy.dma_start(
                                wbuf[:, :],
                                wdram[128 * st:128 * (st + 1), :]),
                                waits=ws_gate[wp] + job.get("wgate", []),
                                inc=wsem, inck=16)
                        else:
                            t_w = pl.op("gpsimd",
                                        lambda g, wbuf=wbuf, st=st,
                                        wdram=wdram: g.dma_start(
                                wbuf[:, :],
                                wdram[128 * st:128 * (st + 1), :]),
                                waits=ws_gate[wp], inc=wsem, inck=16)
                        last_mm = 0
                        if job["kind"] == "feat":
                            for tt_i in range(nct):
                                for oc in range(4):
                                    sl_i = sseq[0] % nslot
                                    sl = psum_slot(sl_i)
                                    sseq[0] += 1

                                    def fm(t, wbuf=wbuf, tt_i=tt_i, oc=oc,
                                           sl=sl, sn=sn):
                                        for c in range(CI):
                                            ins = t.matmul(
                                                sl,
                                                wbuf[:, 512 * c + 128 * oc:
                                                     512 * c +
                                                     128 * (oc + 1)],
                                                A1[:, c * sn + TT * tt_i:
                                                   c * sn + TT * (tt_i + 1)],
                                                start=(c == 0),
                                                stop=(c == CI - 1))
                                        return ins
                                    t_mm = pl.op("tensor", fm,
                                                 waits=[(wsem, t_w),
                                                        (a1_sem, t_a1)] +
                                                 slot_gate[sl_i], inc=mm)
                                    last_mm = t_mm
                                    a1_last[0] = t_mm
                                    ws_gate[wp] = [(mm, t_mm)]
                                    orow = 128 * (4 * st + oc)
                                    ocol = sa0 - a0 + TT * tt_i + \
                                        job.get("ocol0", a0)
                                    out = job["out"]
                                    evac_store(
                                        t_mm,
                                        lambda sc, ev, orow=orow, ocol=ocol,
                                        out=out: sc.dma_start(
                                            out[orow:orow + 128,
                                                ocol:ocol + TT], ev),
                                        gelu=job.get("gelu", False))
                                    yield
                        else:
                            for tt8 in range(sn // 128):
                                sl_i = sseq[0] % nslot
                                sl = psum_slot(sl_i)
                                sseq[0] += 1

                                def fm(t, wbuf=wbuf, tt8=tt8, sl=sl, sn=sn):
                                    for c in range(CI):
                                        ins = t.matmul(
                                            sl,
                                            A1[:, c * sn + 128 * tt8:
                                               c * sn + 128 * (tt8 + 1)],
                                            wbuf[:, 512 * c:512 * (c + 1)],
                                            start=(c == 0),
                                            stop=(c == CI - 1))
                                    return ins
                                t_mm = pl.op("tensor", fm,
                                             waits=[(wsem, t_w),
                                                    (a1_sem, t_a1)] +
                                             slot_gate[sl_i], inc=mm)
                                last_mm = t_mm
                                a1_last[0] = t_mm
                                ws_gate[wp] = [(mm, t_mm)]
                                trow = sa0 + 128 * tt8
                                out = job["out"]
                                evac_store(
                                    t_mm,
                                    lambda sc, ev, trow=trow, st=st, out=out:
                                    sc.dma_start(
                                        out[trow:trow + 128,
                                            512 * st:512 * (st + 1)], ev))
                                yield
                        ws_gate[wp] = [(mm, last_mm)]
                a1_gate = [(mm, a1_last[0])]

    def run_merged(gens_f):
        pl = Plan()
        evt = {}
        drive(pl, gens_f(pl, evt), evt)
        pl.op("sync", lambda sy: sy.nop(), waits=drain_waits())
        pl.run(nc)
        for esl in reversed(evt.get("_esl", [])):
            esl.close()

    CVT = {}

    def cv_mk(pl, name, dst, srcn, sem, pieces=2):
        rows_ = {"wk1b": 512, "wv1b": 512, "wk2b": 512, "wv2b": 512,
                 "wo1b": 512, "wo2b": 512, "wf1b": 2048, "wf2b": 2048}[dst]
        step = rows_ // pieces
        t = None
        for i in range(pieces):
            t = pl.op("gpsimd", lambda g, i=i, dst=dst, srcn=srcn,
                      step=step: g.dma_start(
                T[dst][step * i:step * (i + 1), :],
                T[srcn][step * i:step * (i + 1), :]), inc=sem, inck=16)
            yield
        CVT[name] = (sem, t)

    def gen_d2d(pl, evt):
        def cv(name, dst, srcn, sem, pieces=2):
            n = T[dst].shape[0] if False else None
            rows_ = {"wk1b": 512, "wv1b": 512, "wk2b": 512, "wv2b": 512,
                     "wo1b": 512, "wo2b": 512, "wf1b": 2048, "wf2b": 2048}[dst]
            step = rows_ // pieces
            t = None
            for i in range(pieces):
                t = pl.op("gpsimd", lambda g, i=i, dst=dst, srcn=srcn,
                          step=step: g.dma_start(
                    T[dst][step * i:step * (i + 1), :],
                    T[srcn][step * i:step * (i + 1), :]), inc=sem, inck=16)
                yield
            CVT[name] = (sem, t)
        yield from cv("wk1", "wk1b", "wk1", S_CV[0])
        yield from cv("wv1", "wv1b", "wv1", S_CV[1])

    def gen_d2d_b(pl, evt):
        yield from cv_mk(pl, "wk2", "wk2b", "wk2", S_CV[2])
        yield from cv_mk(pl, "wv2", "wv2b", "wv2", S_CV[3])
        yield from cv_mk(pl, "wo1", "wo1b", "wo1", S_D2D)

    def gen_d2d_b4(pl, evt):
        yield from cv_mk(pl, "wo2", "wo2b", "wo2", S_D2D)
        yield from cv_mk(pl, "wf1", "wf1b", "wf1", S_CV[4], pieces=8)

    def gen_d2d_c(pl, evt):
        yield from cv_mk(pl, "wf2", "wf2b", "wf2", S_D2D, pieces=8)

    def phase_a(pl, evt):
        sec_a = dict(out=T["y16all"], scale_col=V_L1S, bias_col=V_L1B,
                     name="ya")
        sec_o = dict(out=T["y16own"], scale_col=V_L1S, bias_col=V_L1B,
                     name="yo")
        tiles = ([dict(src=T["xt"], i=i, out16=T["xn16all"], name="xa",
                       second=sec_a) for i in range(8)] +
                 [dict(src=T["xto"], i=i, out16=T["xn16own"], name="xo",
                       out32=T["xn32own"], router=True, second=sec_o)
                  for i in range(4)])
        groups = []
        for s in range(4):
            groups.append(dict(
                act=T["xn16all"], a0=1024 * s, an=1024,
                gates=[("st16", "xa", 2 * s), ("st16", "xa", 2 * s + 1)],
                jobs=[dict(kind="feat", w=T["wk1b"], nstrip=4, out=T["kt1"],
                           wbf16=True, wkey="wk1", ocol0=1024 * s),
                      dict(kind="tok", w=T["wv1b"], nstrip=4, out=T["v1"],
                           wbf16=True, wkey="wv1")]))
        groups.append(dict(
            act=T["xn16own"], a0=0, an=TOWN,
            gates=[("st16", "xo", i) for i in range(4)],
            jobs=[dict(kind="feat", w=T["wq1"], nstrip=4, out=T["qt1"],
                       ocol0=0),
                  dict(kind="rank")]))
        return [(gen_ln(pl, evt, tiles), 1),
                (gen_proj(pl, evt, groups, a1_elems=16384), 6),
                (gen_d2d(pl, evt), 1)]

    run_merged(phase_a)
    D2D = [S_D2D.n]


    def gen_attn(pl, evt, sh, qt, kt, vt, wob, out_s32, wob_key=None,
                 ws_bufs=None):
        esl = ExitStack()
        KB = esl.enter_context(nc.sbuf_tensor(uid("kb"), [128, 8192], BF16))
        VB = esl.enter_context(nc.sbuf_tensor(uid("vb"), [128, 8192], BF16))
        QB = esl.enter_context(nc.sbuf_tensor(uid("qb"), [128, 1024], BF16))
        PT = [esl.enter_context(nc.sbuf_tensor(uid("pt"), [128, 1024], BF16))
              for i in range(2)]
        CTX = esl.enter_context(nc.sbuf_tensor(uid("ctx"), [128, 8192], BF16))
        BCS = esl.enter_context(nc.sbuf_tensor(uid("bcs"), [128, 1024], F32))
        EVF = esl.enter_context(nc.sbuf_tensor(uid("evf"), [128, 1024], F32))
        XR = (esl.enter_context(nc.sbuf_tensor(uid("xr"), [128, 1024], F32))
              if not out_s32 else None)
        evt.setdefault("_esl", []).append(esl)
        if ws_bufs is None:
            ws_bufs = [(WS[0], S_WS[0]), (WS[1], S_WS[1])]
        ws_gate = [[] for _ in ws_bufs]
        wseq = [0]

        bank = {i: [] for i in range(4)}
        kqv_gate = []
        pt_gate = [[], []]
        ctxb_gate = [[], []]
        den_gate = []
        evf_gate = [[], []]
        xr_gate = [[], []]
        bcs_gate = []
        ctx_sb_gate = []

        for tk_i in range(4):
            nk = 8 * (tk_i + 1)
            last_cm = 0
            for pr in range(8):
                h0, h1 = 2 * pr, 2 * pr + 1

                def fk(sy, h0=h0, h1=h1, nk=nk, tk_i=tk_i):
                    sy.dma_start(KB[:, 0:128 * nk],
                                 kt[128 * h0:128 * (h0 + 1), 0:128 * nk]
                                 ).then_inc(S_LD[0].h, 16)
                    sy.dma_start(KB[:, 4096:4096 + 128 * nk],
                                 kt[128 * h1:128 * (h1 + 1), 0:128 * nk]
                                 ).then_inc(S_LD[0].h, 16)
                    sy.dma_start(QB[:, 0:512],
                                 qt[128 * h0:128 * (h0 + 1),
                                    TT * tk_i:TT * (tk_i + 1)]
                                 ).then_inc(S_LD[0].h, 16)
                    return sy.dma_start(QB[:, 512:1024],
                                        qt[128 * h1:128 * (h1 + 1),
                                           TT * tk_i:TT * (tk_i + 1)])
                S_LD[0].n += 48
                t_k = pl.op("sync", fk, waits=kqv_gate, inc=S_LD[0], inck=16)
                t_v = pl.op("sync", lambda sy, pr=pr, nk=nk: sy.dma_start(
                    VB[:, 0:256 * nk].rearrange("p (j f) -> p j f", j=nk),
                    vt[0:128 * nk, 256 * pr:256 * (pr + 1)].rearrange(
                        "(j p) f -> p j f", p=128)), inc=S_LD[0], inck=16)

                t_exp = [0] * nk
                t_pv = [0] * nk
                t_s = [0] * nk
                for j in range(nk + 1):
                    if j < nk:
                        sp = j % 2
                        d = j - (nk - 8)

                        def fs(t, j=j, sp=sp, d=d):
                            if d >= 0:
                                t.matmul(PSW[sp][:, 0:512], IDN[:, :],
                                         CMK[:, 512 * d:512 * (d + 1)],
                                         start=True, stop=False)
                                t.matmul(PSW[sp][:, 512:1024], IDN[:, :],
                                         CMK[:, 512 * d:512 * (d + 1)],
                                         start=True, stop=False)
                            t.matmul(PSW[sp][:, 0:512],
                                     KB[:, 128 * j:128 * (j + 1)],
                                     QB[:, 0:512], start=(d < 0), stop=True)
                            return t.matmul(
                                PSW[sp][:, 512:1024],
                                KB[:, 4096 + 128 * j:4096 + 128 * (j + 1)],
                                QB[:, 512:1024], start=(d < 0), stop=True)
                        t_s[j] = pl.op("tensor", fs,
                                       waits=[(S_LD[0], t_k)] +
                                       bank[2 * sp] + bank[2 * sp + 1],
                                       inc=mm)
                        ewait = [(mm, t_s[j])]
                        jp = j % 2
                        t_exp[j] = pl.op("scalar", lambda sc, jp=jp, sp=sp:
                                         sc.activation(PT[jp][:, :],
                                                       PSW[sp][:, :], AF.Exp,
                                                       scale=SCALE),
                                         waits=ewait + pt_gate[jp], inc=ex)
                        bank[2 * sp] = bank[2 * sp + 1] = [(ex, t_exp[j])]
                    jj = j - 1
                    if jj < 0:
                        yield
                        continue
                    jp = jj % 2

                    def fpv(t, jj=jj, jp=jp, nk=nk):
                        t.matmul(PSW[2][:, 0:512],
                                 VB[:, 256 * jj:256 * jj + 128],
                                 PT[jp][:, 0:512], start=(jj == 0),
                                 stop=(jj == nk - 1))
                        t.matmul(PSW[2][:, 512:1024],
                                 VB[:, 256 * jj + 128:256 * (jj + 1)],
                                 PT[jp][:, 512:1024], start=(jj == 0),
                                 stop=(jj == nk - 1))
                        t.matmul(PSW[3][0:1, 0:512], ONESC[:, :],
                                 PT[jp][:, 0:512], start=(jj == 0),
                                 stop=(jj == nk - 1), skip_group_check=True)
                        return t.matmul(PSW[3][32:33, 0:512],
                                        ONESC[:, :], PT[jp][:, 512:1024],
                                        start=(jj == 0), stop=(jj == nk - 1),
                                        tile_position=(0, 32),
                                        skip_group_check=True)
                    w0 = [(ex, t_exp[jj]), (S_LD[0], t_v)]
                    if jj == 0:
                        w0 += ctxb_gate[0] + ctxb_gate[1] + den_gate
                    t_pv[jj] = pl.op("tensor", fpv, waits=w0, inc=mm)
                    pt_gate[jp] = [(mm, t_pv[jj])]
                    yield
                kqv_gate = [(mm, t_pv[nk - 1])]

                def frc(sc):
                    sc.activation(rows(R_L1), PSW[3][0:1, 0:512], AF.Ln)
                    sc.activation(rows(R_L2), PSW[3][32:33, 0:512], AF.Ln)
                    sc.activation(rows(R_L1), rows(R_L1), AF.Exp, scale=-1.0)
                    return sc.activation(rows(R_L2), rows(R_L2), AF.Exp,
                                         scale=-1.0)
                t_rc = pl.op("scalar", frc, waits=[(mm, t_pv[nk - 1])], inc=ex)
                den_gate = [(ex, t_rc)]

                def fbc2(t):
                    t.matmul(PSW[0][:, 0:512], ONESR[:, :], rows(R_L1),
                             start=True, stop=True)
                    return t.matmul(PSW[0][:, 512:1024], ONESR[:, :],
                                    rows(R_L2), start=True, stop=True)
                t_bc = pl.op("tensor", fbc2,
                             waits=[(ex, t_rc)] + bank[0] + bank[1], inc=mm)
                t_bcc = pl.op("scalar", lambda sc: sc.copy(
                    BCS[:, :], PSW[0][:, :]),
                    waits=[(mm, t_bc)] + bcs_gate, inc=ex)
                bank[0] = bank[1] = [(ex, t_bcc)]

                def fcm(v, h0=h0, h1=h1):
                    v.tensor_mul(CTX[:, 512 * h0:512 * (h0 + 1)],
                                 PSW[2][:, 0:512], BCS[:, 0:512])
                    return v.tensor_mul(CTX[:, 512 * h1:512 * (h1 + 1)],
                                        PSW[2][:, 512:1024],
                                        BCS[:, 512:1024])
                t_cm = pl.op("vector", fcm,
                             waits=[(ex, t_bcc), (mm, t_pv[nk - 1])] +
                             ctx_sb_gate, inc=ve)
                ctxb_gate[0] = ctxb_gate[1] = [(ve, t_cm)]
                bcs_gate = [(ve, t_cm)]
                last_cm = t_cm
                yield

            oproj_last = 0
            if wob_key is not None:
                while wob_key not in CVT:
                    yield ('need', ("cvt", wob_key))
                sh["d2d"] = CVT[wob_key][1]
            for st in range(4):
                wp = wseq[0] % len(ws_bufs)
                wseq[0] += 1
                wbuf, wsem = ws_bufs[wp]
                t_w = pl.op("sync", lambda sy, wbuf=wbuf, st=st: sy.dma_start(
                    wbuf[:, :], wob[128 * st:128 * (st + 1), :]),
                    waits=ws_gate[wp] + [(S_D2D, sh["d2d"])],
                    inc=wsem, inck=16)
                for oc in range(4):
                    si = (st * 4 + oc) % 4
                    sl = PSW[si // 2][:, 512 * (si % 2):512 * (si % 2 + 1)]

                    def fo(t, wbuf=wbuf, oc=oc, sl=sl):
                        for c in range(CI):
                            ins = t.matmul(
                                sl, wbuf[:, 512 * c + 128 * oc:
                                         512 * c + 128 * (oc + 1)],
                                CTX[:, 512 * c:512 * (c + 1)],
                                start=(c == 0), stop=(c == CI - 1))
                        return ins
                    t_mm = pl.op("tensor", fo,
                                 waits=[(wsem, t_w), (ve, last_cm)] +
                                 bank[si], inc=mm)
                    oproj_last = t_mm
                    ws_gate[wp] = [(mm, t_mm)]
                    ei = (st * 4 + oc) % 2
                    orow = 128 * (4 * st + oc)
                    ocol = TT * tk_i
                    if out_s32:
                        t_ev = pl.op("vector", lambda v, sl=sl, ei=ei:
                                     v.tensor_copy(
                            EVF[:, 512 * ei:512 * (ei + 1)], sl),
                            waits=[(mm, t_mm)] + evf_gate[ei], inc=ve)
                        dst = T["s32"]
                    else:
                        t_x = pl.op("sync", lambda sy, ei=ei, orow=orow,
                                    ocol=ocol: sy.dma_start(
                            XR[:, 512 * ei:512 * (ei + 1)],
                            T["xn32own"][orow:orow + 128, ocol:ocol + 512]),
                            waits=xr_gate[ei], inc=S_XR[ei], inck=16)
                        t_ev = pl.op("vector", lambda v, sl=sl, ei=ei:
                                     v.tensor_add(
                            EVF[:, 512 * ei:512 * (ei + 1)], sl,
                            XR[:, 512 * ei:512 * (ei + 1)]),
                            waits=[(mm, t_mm), (S_XR[ei], t_x)] +
                            evf_gate[ei], inc=ve)
                        xr_gate[ei] = [(ve, t_ev)]
                        dst = T["u32"]
                    bank[si] = [(ve, t_ev)]
                    t_st = pl.op("scalar", lambda sc, ei=ei, orow=orow,
                                 ocol=ocol, dst=dst: sc.dma_start(
                        dst[orow:orow + 128, ocol:ocol + 512],
                        EVF[:, 512 * ei:512 * (ei + 1)]),
                        waits=[(ve, t_ev)], inc=S_EV[ei], inck=16)
                    evf_gate[ei] = [(S_EV[ei], t_st)]
                    yield
            ctx_sb_gate = [(mm, oproj_last)]

    def groups3_mk():
        groups3 = []
        for s in range(2):
            groups3.append(dict(
                act=T["y16all"], a0=2048 * s, an=2048,
                jobs=[dict(kind="feat", w=T["wk2b"], nstrip=4, out=T["kt2"],
                           wbf16=True, wkey="wk2",
                           ocol0=2048 * s),
                      dict(kind="tok", w=T["wv2b"], nstrip=4, out=T["v2"],
                           wbf16=True, wkey="wv2")]))
        groups3.append(dict(
            act=T["y16own"], a0=0, an=TOWN,
            jobs=[dict(kind="feat", w=T["wq2"], nstrip=4, out=T["qt2"],
                       ocol0=0)]))
        return groups3

    def phase_b(pl, evt):
        sh = dict(d2d=0)
        return [(gen_d2d_b(pl, evt), 1),
                (gen_attn(pl, evt, sh, T["qt1"], T["kt1"], T["v1"],
                          T["wo1b"], True, wob_key="wo1"), 8)]

    run_merged(phase_b)

    def phase_b3(pl, evt):
        return [(gen_proj(pl, evt, groups3_mk(), a1_elems=32768), 1)]

    run_merged(phase_b3)

    def phase_b4(pl, evt):
        sh = dict(d2d=0)
        return [(gen_d2d_b4(pl, evt), 1),
                (gen_attn(pl, evt, sh, T["qt2"], T["kt2"], T["v2"],
                          T["wo2b"], False, wob_key="wo2"), 8)]

    run_merged(phase_b4)


    def phase_c(pl, evt):
        tiles = [dict(src=T["u32"], i=i, out16=T["z16"], name="z",
                      sb_cols=(V_L2S, V_L2B)) for i in range(4)]
        groups6 = []
        for s in range(2):
            groups6.append(dict(
                act=T["z16"], a0=1024 * s, an=1024,
                gates=[("st16", "z", 2 * s), ("st16", "z", 2 * s + 1)],
                jobs=[dict(kind="feat", w=T["wf1b"], nstrip=16, out=T["h16"],
                           gelu=True, wbf16=True, wkey="wf1",
                           ocol0=1024 * s)]))
        return [(gen_d2d_c(pl, evt), 1), (gen_ln(pl, evt, tiles), 1),
                (gen_proj(pl, evt, groups6, a1_elems=16384), 6)]

    run_merged(phase_c)
    D2D[0] = S_D2D.n


    def final_phase():
        pl = Plan()
        esl = ExitStack()
        HB = esl.enter_context(nc.sbuf_tensor(uid("hb"), [128, 32768], BF16))
        MB = esl.enter_context(nc.sbuf_tensor(uid("mb"), [128, 2 * 512],
                                               mybir.dt.uint8))
        FA = esl.enter_context(nc.sbuf_tensor(uid("fa"), [128, 16 * 512],
                                               F32))
        FO = esl.enter_context(nc.sbuf_tensor(uid("fo"), [128, 2 * 512], F32))
        ws_gate = [[], []]
        fo_gate = [[], []]
        hb_gate = []
        mb_gate = []
        fa_gate = [[], []]
        slot_gate = [[], []]
        lr_gate = []
        rows_gate = []

        for tt_i in range(4):
            tc = slice(TT * tt_i, TT * (tt_i + 1))
            t_h = pl.op("sync", lambda sy, tc=tc: sy.dma_start(
                HB[:, :].rearrange("p (c t) -> p c t", c=64),
                T["h16"][:, tc].rearrange("(c p) t -> p c t", p=128)),
                waits=hb_gate, inc=S_A1, inck=16)
            pl.op("sync", lambda sy, tt_i=tt_i: sy.dma_start(
                rows(R_L1), T["mrows"][0:1, TT * tt_i:TT * (tt_i + 1)]),
                waits=rows_gate, inc=S_ROW, inck=16)
            t_mr = pl.op("sync", lambda sy, tt_i=tt_i: sy.dma_start(
                rows(R_L2),
                T["mrows"][0:1, TOWN + TT * tt_i:TOWN + TT * (tt_i + 1)]),
                inc=S_ROW, inck=16)

            def fmb(t):
                t.matmul(PSW[3][:, 0:512], ONESR[:, :], rows(R_L1),
                         start=True, stop=True)
                return t.matmul(PSW[3][:, 512:1024], ONESR[:, :], rows(R_L2),
                                start=True, stop=True)
            t_mb = pl.op("tensor", fmb, waits=[(S_ROW, t_mr)] + mb_gate,
                         inc=mm)
            t_mbc = pl.op("scalar", lambda sc: (
                sc.copy(MB[:, 0:512], PSW[3][:, 0:512]),
                sc.copy(MB[:, 512:1024], PSW[3][:, 512:1024]))[-1],
                waits=[(mm, t_mb)] + mb_gate, inc=ex)
            rows_gate = [(mm, t_mb)]
            mb_next = []
            for oc in range(CI):
                wp = (tt_i * CI + oc) % 2
                t_w = pl.op("sync", lambda sy, wp=wp, oc=oc: sy.dma_start(
                    WS[wp][:, :], T["wf2b"][128 * oc:128 * (oc + 1), :]),
                    waits=ws_gate[wp] + [(S_D2D, D2D[0])], inc=S_WS[wp],
                    inck=16)
                si = oc % 2
                sl = PSW[si][:, 0:512]

                def fm(t, wp=wp, sl=sl):
                    for c in range(64):
                        ins = t.matmul(sl, WS[wp][:, 128 * c:128 * (c + 1)],
                                       HB[:, 512 * c:512 * (c + 1)],
                                       start=(c == 0), stop=(c == 63))
                    return ins
                t_mm2 = pl.op("tensor", fm,
                              waits=[(S_WS[wp], t_w), (S_A1, t_h)] +
                              slot_gate[si], inc=mm)
                ws_gate[wp] = [(mm, t_mm2)]
                slr = PSW[2][:, 0:512]

                def flr(t, oc=oc, tc=tc):
                    return t.matmul(slr, WUP[:, 128 * oc:128 * (oc + 1)],
                                    T1[:, tc], start=True, stop=True)
                t_lr = pl.op("tensor", flr, waits=lr_gate, inc=mm)
                orow = slice(128 * oc, 128 * (oc + 1))
                fop = (tt_i * CI + oc) % 2
                B0 = 4096 * fop

                def fld(sy, orow=orow, tc=tc, B0=B0):
                    sy.dma_start(FA[:, B0:B0 + 512],
                                 T["xn32own"][orow, tc]).then_inc(S_FL.h, 16)
                    sy.dma_start(FA[:, B0 + 512:B0 + 1024],
                                 T["u32"][orow, tc]).then_inc(S_FL.h, 16)
                    sy.dma_start(FA[:, B0 + 1024:B0 + 1536],
                                 T["xto"][orow, tc]).then_inc(S_FL.h, 16)
                    return sy.dma_start(FA[:, B0 + 1536:B0 + 2048],
                                        T["s32"][orow, tc])
                S_FL.n += 48
                t_fl = pl.op("sync", fld, waits=fa_gate[fop], inc=S_FL,
                             inck=16)

                def fev(v, sl=sl, oc=oc, B0=B0):
                    v.tensor_add(FA[:, B0 + 2048:B0 + 2560], sl,
                                 FA[:, B0 + 512:B0 + 1024])
                    v.tensor_add(FA[:, B0 + 2560:B0 + 3072], slr,
                                 FA[:, B0:B0 + 512])
                    v.tensor_scalar(FA[:, B0 + 3072:B0 + 3584],
                                    FA[:, B0:B0 + 512],
                                    VECS[:, V_SB * 16 + oc:V_SB * 16 + oc + 1],
                                    None, ALU.add)
                    v.select(FA[:, B0 + 3584:B0 + 4096], MB[:, 512:1024],
                             FA[:, B0 + 2560:B0 + 3072],
                             FA[:, B0 + 2048:B0 + 2560])
                    return v.select(FA[:, B0 + 2048:B0 + 2560], MB[:, 0:512],
                                    FA[:, B0 + 3072:B0 + 3584],
                                    FA[:, B0 + 3584:B0 + 4096])
                t_ev = pl.op("vector", fev,
                             waits=[(mm, t_mm2), (mm, t_lr), (S_FL, t_fl),
                                    (ex, t_mbc)], inc=ve)
                slot_gate[si] = [(ve, t_ev)]
                lr_gate = [(ve, t_ev)]
                mb_next = [(ve, t_ev)]
                t_g = pl.op("scalar", lambda sc, B0=B0: sc.activation(
                    FA[:, B0 + 3584:B0 + 4096],
                    FA[:, B0 + 2048:B0 + 2560], AF.Silu),
                    waits=[(ve, t_ev)], inc=ex)

                def ffin(v, fop=fop, B0=B0):
                    v.tensor_mul(FA[:, B0 + 2048:B0 + 2560],
                                 FA[:, B0 + 3584:B0 + 4096],
                                 FA[:, B0 + 1536:B0 + 2048])
                    return v.tensor_add(FO[:, 512 * fop:512 * (fop + 1)],
                                        FA[:, B0 + 2048:B0 + 2560],
                                        FA[:, B0 + 1024:B0 + 1536])
                t_f = pl.op("vector", ffin,
                            waits=[(ex, t_g)] + fo_gate[fop], inc=ve)
                fa_gate[fop] = [(ve, t_f)]
                t_st = pl.op("scalar", lambda sc, fop=fop, orow=orow, tc=tc:
                             sc.dma_start(T["outT"][orow, tc],
                                          FO[:, 512 * fop:512 * (fop + 1)]),
                             waits=[(ve, t_f)], inc=S_EV[fop], inck=16)
                fo_gate[fop] = [(S_EV[fop], t_st)]
            hb_gate = [(mm, mm.n)]
            mb_gate = mb_next
        pl.op("sync", lambda sy: sy.nop(), waits=drain_waits())
        pl.run(nc)
        esl.close()

    final_phase()
    return nc, es


def _block_w(wt, nstrip, wo):
    din, dout = wt.shape
    ci = din // 128
    a = wt.reshape(ci, 128, nstrip, wo).transpose(2, 1, 0, 3)
    return np.ascontiguousarray(a.reshape(nstrip * 128, ci * wo))


def _pack_vec(v):
    return np.ascontiguousarray(v.reshape(16, 128).T)


def kernel(L0_attn_hidden_states, L0_attn_w_qkv, L0_attn_w_o,
           L0_ffn_simple_bias, L0_ffn_w_down, L0_ffn_w_up, L0_ffn_w_router,
           L0_ffn_ln1_scale, L0_ffn_ln1_bias, L0_ffn_wq, L0_ffn_wk,
           L0_ffn_wv, L0_ffn_wo, L0_ffn_ln2_scale, L0_ffn_ln2_bias,
           L0_ffn_w_ff1, L0_ffn_w_ff2):
    f32 = lambda a: np.asarray(a, dtype=np.float32)
    x = f32(L0_attn_hidden_states)
    qkv = f32(L0_attn_w_qkv)

    wq1 = _block_w(np.ascontiguousarray(qkv[0:D].T), 4, 512)
    wk1 = _block_w(np.ascontiguousarray(qkv[D:2 * D].T), 4, 512)
    wv1 = _block_w(np.ascontiguousarray(qkv[2 * D:3 * D].T), 4, 512)
    wo1 = _block_w(np.ascontiguousarray(f32(L0_attn_w_o).T), 4, 512)
    wq2 = _block_w(np.ascontiguousarray(f32(L0_ffn_wq).T), 4, 512)
    wk2 = _block_w(np.ascontiguousarray(f32(L0_ffn_wk).T), 4, 512)
    wv2 = _block_w(np.ascontiguousarray(f32(L0_ffn_wv).T), 4, 512)
    wo2 = _block_w(np.ascontiguousarray(f32(L0_ffn_wo).T), 4, 512)
    wf1 = _block_w(np.ascontiguousarray(f32(L0_ffn_w_ff1).T), 16, 512)
    wf2 = _block_w(np.ascontiguousarray(f32(L0_ffn_w_ff2).T), 16, 128)

    vecs = np.stack([_pack_vec(f32(v)) for v in
                     [L0_ffn_simple_bias, L0_ffn_ln1_scale, L0_ffn_ln1_bias,
                      L0_ffn_ln2_scale, L0_ffn_ln2_bias]], axis=1)
    vecs = np.ascontiguousarray(vecs.reshape(128, 80))
    wrt = np.ascontiguousarray(
        f32(L0_ffn_w_router).T.reshape(16, 128, 3).transpose(1, 0, 2)
        .reshape(128, 48))
    wdt = np.ascontiguousarray(
        f32(L0_ffn_w_down).T.reshape(16, 128, RK).transpose(1, 0, 2)
        .reshape(128, 16 * RK))
    wupt = np.ascontiguousarray(f32(L0_ffn_w_up).T)

    cmasks = []
    for par in range(2):
        p = np.arange(128)[:, None]
        f = np.arange(512)[None, :]
        m = np.empty((128, 8 * 512), dtype=np.float32)
        for d in range(8):
            allowed = (128 * d + p) <= (128 * (2 * (f // 128) + par) + f % 128)
            m[:, 512 * d:512 * (d + 1)] = np.where(allowed, 0.0, NEG)
        cmasks.append(m.astype(ml_dtypes.bfloat16))

    nc, es = build_program()
    shared = dict(vecs=vecs, wrt=wrt, wdt=wdt, wupt=wupt,
                  wq1=wq1, wk1=wk1, wv1=wv1, wo1=wo1, wq2=wq2, wk2=wk2,
                  wv2=wv2, wo2=wo2, wf1=wf1, wf2=wf2)
    in_maps = []
    for c in range(8):
        b, par = c // 2, c % 2
        gsl = np.concatenate(
            [np.arange(128 * (2 * L + par), 128 * (2 * L + par) + 128)
             for L in range(16)])
        m = dict(shared)
        m["xt"] = np.ascontiguousarray(x[b].T)
        m["xto"] = np.ascontiguousarray(x[b][gsl].T)
        m["cmask"] = cmasks[par]
        in_maps.append(m)

    res = run_bass_kernel_spmd(nc, in_maps, core_ids=list(range(8)))
    es.close()

    out = np.empty((B, S, D), dtype=np.float32)
    for c in range(8):
        b, par = c // 2, c % 2
        ot = res.results[c]["outT"]
        for L in range(16):
            g = 2 * L + par
            out[b, 128 * g:128 * (g + 1), :] = \
                ot[:, 128 * L:128 * (L + 1)].T
    return out


if __name__ == "__main__":
    ncx, esx = build_program()
    print("built ok")
    esx.close()
